# version 04
# baseline (speedup 1.0000x reference)
import numpy as np

try:
    import concourse.bass as bass
except ImportError:
    import sys
    sys.path.insert(0, "/opt/trn_rl_repo")
    import concourse.bass as bass
import concourse.mybir as mybir
from concourse.bass import IndirectOffsetOnAxis
from concourse.bass_utils import run_bass_kernel_spmd
import bass_rust
from concourse.engine_type import EngineType

import collections
import re

_ENG_ET = {"SP": EngineType.SP, "Activation": EngineType.Activation,
           "PE": EngineType.PE, "DVE": EngineType.DVE, "Pool": EngineType.Pool}


def _free_dyn_regs(nc, prefix):
    tail = collections.deque(nc.all_instructions(), maxlen=6)
    pat = re.compile(r"R\[(" + prefix + r"_[A-Za-z0-9_]+)\]")
    for ins in tail:
        for nm in set(pat.findall(str(ins))):
            if "_snap_" in nm or nm.startswith(prefix + "_tmp_"):
                try:
                    nc.free_register(bass_rust.RegisterHandle(nm, _ENG_ET[prefix]))
                except Exception:
                    pass


from contextlib import contextmanager


@contextmanager
def _fori(nc, eng, prefix, start, end):
    n0 = sum(1 for _ in nc.all_instructions())
    with eng.Fori(start, end) as i:
        yield i
    pat = re.compile(r"R\[(" + prefix + r"_[A-Za-z0-9_]*_fori_[A-Za-z0-9_]+)\]")
    names = set()
    for k, ins in enumerate(nc.all_instructions()):
        if k < n0:
            continue
        names |= set(pat.findall(str(ins)))
    for nm in names:
        try:
            nc.free_register(bass_rust.RegisterHandle(nm, _ENG_ET[prefix]))
        except Exception:
            pass

F32 = mybir.dt.float32
BF16 = mybir.dt.bfloat16
I32 = mybir.dt.int32
U32 = mybir.dt.uint32
AF = mybir.ActivationFunctionType
ALU = mybir.AluOpType
AXX = mybir.AxisListType.X

NCORES = 8
NT = 4096
NTILE = 32
NTOK = 32768
NBLK = 256
D = 1024
FF = 4096
E = 8
CAP = 6144
SECT = 4096
ETILE = 256
S = 8192
NP = 8
NQT = 4
NKC = 64
BIG = 3000000.0
ARN = 98304

PE_T = 66
LDF_T = 16
SCF_T = 16


def build_nc(stage=3):
    nc = bass.Bass("TRN2", target_bir_lowering=False)

    x_in = nc.dram_tensor("x_in", [NT, D], F32, kind="ExternalInput").ap()
    Wq = nc.dram_tensor("Wq", [D, D], F32, kind="ExternalInput").ap()
    Wk = nc.dram_tensor("Wk", [D, D], F32, kind="ExternalInput").ap()
    Wv = nc.dram_tensor("Wv", [D, D], F32, kind="ExternalInput").ap()
    Wo = nc.dram_tensor("Wo", [D, D], F32, kind="ExternalInput").ap()
    bq = nc.dram_tensor("bq", [128, 8], F32, kind="ExternalInput").ap()
    bk = nc.dram_tensor("bk", [128, 8], F32, kind="ExternalInput").ap()
    bv = nc.dram_tensor("bv", [1, D], F32, kind="ExternalInput").ap()
    bo = nc.dram_tensor("bo", [1, D], F32, kind="ExternalInput").ap()
    rw = nc.dram_tensor("rw", [E, D], F32, kind="ExternalInput").ap()
    w1a = nc.dram_tensor("w1a", [E, D, FF], F32, kind="ExternalInput").ap()
    b1a = nc.dram_tensor("b1a", [1, E * FF], F32, kind="ExternalInput").ap()
    w2a = nc.dram_tensor("w2a", [E, FF, D], F32, kind="ExternalInput").ap()
    b2a = nc.dram_tensor("b2a", [1, E * D], F32, kind="ExternalInput").ap()
    ccin = nc.dram_tensor("ccin", [1, 8], F32, kind="ExternalInput").ap()
    out = nc.dram_tensor("out", [NT, D], F32, kind="ExternalOutput").ap()

    wT_q = nc.dram_tensor("wT_q", [D, D], BF16).ap()
    wT_k = nc.dram_tensor("wT_k", [D, D], BF16).ap()
    wT_v = nc.dram_tensor("wT_v", [D, D], BF16).ap()
    wT_o = nc.dram_tensor("wT_o", [D, D], BF16).ap()
    ln_dram = nc.dram_tensor("ln_dram", [NT, D], BF16).ap()
    xg_dram = nc.dram_tensor("xg_dram", [E * SECT + ETILE, D], BF16).ap()
    eo_dram = nc.dram_tensor("eo_dram", [E * SECT + ETILE, D], BF16).ap()
    x_sp1 = nc.dram_tensor("x_sp1", [NT, D], F32).ap()
    x_sp2 = nc.dram_tensor("x_sp2", [NT, D], F32).ap()
    qt_dram = nc.dram_tensor("qt_dram", [NP * 128, NT], BF16).ap()
    pos_dram = nc.dram_tensor("pos_dram", [NBLK, 128], F32).ap()
    e_dram = nc.dram_tensor("e_dram", [NBLK, 128], F32).ap()
    p_dram = nc.dram_tensor("p_dram", [128, NBLK], F32).ap()
    B_dram = nc.dram_tensor("B_dram", [NBLK + 1, E], F32).ap()
    lg_in = [nc.dram_tensor(f"lg_in{a}", [NT, E], F32).ap() for a in (0, 1)]
    lg_out = [nc.dram_tensor(f"lg_out{a}", [NTOK, E], F32).ap() for a in (0, 1)]
    kt_in = nc.dram_tensor("kt_in", [D, NT], BF16).ap()
    kt_out = nc.dram_tensor("kt_out", [2 * D, NT], BF16).ap()
    v_in = nc.dram_tensor("v_in", [NT, D], BF16).ap()
    v_out = nc.dram_tensor("v_out", [S, D], BF16).ap()

    from contextlib import ExitStack
    es = ExitStack()
    ec = es.enter_context

    arena = ec(nc.sbuf_tensor("arena", [128, ARN], BF16))
    ident_bf = ec(nc.sbuf_tensor("ident_bf", [128, 128], BF16))
    ident_f = ec(nc.sbuf_tensor("ident_f", [128, 128], F32))
    lt_bf = ec(nc.sbuf_tensor("lt_bf", [128, 128], BF16))
    iota_col = ec(nc.sbuf_tensor("iota_col", [128, 1], F32))
    iota_row = ec(nc.sbuf_tensor("iota_row", [128, 128], F32))
    ones_col_bf = ec(nc.sbuf_tensor("ones_col_bf", [128, 1], BF16))
    ones_row_bf = ec(nc.sbuf_tensor("ones_row_bf", [1, 512], BF16))
    ones64_f = ec(nc.sbuf_tensor("ones64_f", [1, 64], F32))
    onesf_row = ec(nc.sbuf_tensor("onesf_row", [1, 128], F32))
    zeros128 = ec(nc.sbuf_tensor("zeros128", [128, 128], F32))
    cc_f = ec(nc.sbuf_tensor("cc_f", [1, 8], F32))
    cc_i = ec(nc.sbuf_tensor("cc_i", [1, 8], I32))
    bq_sb = ec(nc.sbuf_tensor("bq_sb", [128, 8], F32))
    bk_sb = ec(nc.sbuf_tensor("bk_sb", [128, 8], F32))
    rwT = ec(nc.sbuf_tensor("rwT", [128, E * 8], BF16))
    cnts_f = ec(nc.sbuf_tensor("cnts_f", [1, E], F32))
    cnts_i = ec(nc.sbuf_tensor("cnts_i", [1, E], I32))
    scat_u = ec(nc.sbuf_tensor("scat_u", [128, NTILE], U32))
    gath_u = ec(nc.sbuf_tensor("gath_u", [128, NTILE], U32))
    gsc_sb = ec(nc.sbuf_tensor("gsc_sb", [128, NTILE], F32))

    ps = ec(nc.psum_tensor("ps", [128, 4096], F32))

    semnames = ("s_bar", "s_rel", "s_ld", "s_lda", "s_ldb", "s_ldf", "s_sc", "s_scf", "s_g", "s_g2",
                "s_w1", "s_w2", "s_cc", "s_pe",
                "s_act", "s_dve", "s_gp", "s_pef", "s_actf", "s_dvef", "s_dvef2")
    sems = {n: ec(nc.semaphore(n)) for n in semnames}
    block = ec(nc.Block())
    cnt = {n: 0 for n in semnames}

    def inc(instr, name, n=1):
        cnt[name] += n
        return instr.then_inc(sems[name], n)

    def w(eng, name, val=None):
        eng.wait_ge(sems[name], cnt[name] if val is None else val)

    def av(off, parts, cols, dtype=BF16):
        width = cols * (2 if dtype in (F32, I32, U32) else 1)
        vw = arena[0:parts, off:off + width]
        if dtype != BF16:
            vw = vw.bitcast(dtype)
        return vw

    regs = {}
    def reg(eng, name):
        key = (eng.engine, name)
        if key not in regs:
            regs[key] = eng.alloc_register(name + '_r')
        return regs[key]

    barno = [0]

    def barrier():
        barno[0] += 1
        k = barno[0]

        @block.tensor
        def _(t_):
            t_.nop().then_inc(sems["s_bar"], 1)

        @block.scalar
        def _(s_):
            w(s_, "s_sc")
            s_.nop().then_inc(sems["s_bar"], 1)

        @block.vector
        def _(v_):
            v_.nop().then_inc(sems["s_bar"], 1)

        @block.gpsimd
        def _(g_):
            w(g_, "s_g"); w(g_, "s_g2"); w(g_, "s_w1"); w(g_, "s_w2")
            g_.nop().then_inc(sems["s_bar"], 1)

        @block.sync
        def _(sy):
            w(sy, "s_ld"); w(sy, "s_lda"); w(sy, "s_ldb")
            sy.wait_ge(sems["s_bar"], 4 * k)
            sy.nop().then_inc(sems["s_rel"], 1)

        for eng in ("tensor", "scalar", "vector", "gpsimd"):
            @getattr(block, eng)
            def _(e_):
                e_.wait_ge(sems["s_rel"], k)

    trps = ps[:, 3072:3584].bitcast(BF16)
    ASCR0 = 40960
    FS = 53248
    def fsv(k, parts, n, dtype=F32):
        return av(FS + 2 * k, parts, n, dtype)
    lnrot = [av(45056, 128, 1024), av(46080, 128, 1024)]
    ztile = av(ASCR0 + 7168, 128, 2048)
    iota8 = fsv(16000, 128, 2048)
    iota8m8 = fsv(18048, 128, 2048)

    def scope(name):
        return nc.enter_named_scope(name, False)[0]

    def escope(name, sid):
        nc.leave_named_scope(name, sid, False)

    _sid0 = scope("pre")
    @block.sync
    def _(sy):
        inc(sy.dma_start(out=cc_f[:], in_=ccin), "s_ld", 16)
        inc(sy.dma_start(out=bq_sb[:], in_=bq), "s_ld", 16)
        inc(sy.dma_start(out=bk_sb[:], in_=bk), "s_ld", 16)

    @block.gpsimd
    def _(g):
        g.reg_mov(reg(g, "bchk"), E * SECT + ETILE - 1)
        inc(g.iota(iota_col[:], pattern=[[0, 1]], base=0, channel_multiplier=1,
                   allow_small_or_imprecise_dtypes=True), "s_gp")
        inc(g.iota(iota_row[:], pattern=[[1, 128]], base=0, channel_multiplier=0,
                   allow_small_or_imprecise_dtypes=True), "s_gp")

    @block.vector
    def _(v):
        w(v, "s_gp")
        w(v, "s_ld")
        inc(v.tensor_scalar(ident_bf[:], iota_row[:], iota_col[:], None, ALU.is_equal), "s_dve")
        inc(v.tensor_scalar(ident_f[:], iota_row[:], iota_col[:], None, ALU.is_equal), "s_dve")
        inc(v.tensor_scalar(lt_bf[:], iota_row[:], iota_col[:], None, ALU.is_gt), "s_dve")
        inc(v.memset(ones_col_bf[:], 1.0), "s_dve")
        inc(v.memset(ones_row_bf[:], 1.0), "s_dve")
        inc(v.memset(ones64_f[:], 1.0), "s_dve")
        inc(v.memset(onesf_row[:], 1.0), "s_dve")
        inc(v.memset(zeros128[:], 0.0), "s_dve")
        inc(v.memset(ztile[:], 0.0), "s_dve")
        inc(v.tensor_copy(cc_i[:], cc_f[:]), "s_dve")

    @block.scalar
    def _(s):
        w(s, "s_dve")
        nrows = E * SECT + ETILE
        for dst in (xg_dram, eo_dram):
            for i in range(0, nrows, 256):
                n = min(256, nrows - i)
                inc(s.dma_start(out=dst[i:i + n, :].rearrange("(a p) d -> p a d", p=128),
                                in_=ztile[:].rearrange("p (a d) -> p a d", d=D)[:, 0:n // 128, :]),
                    "s_sc", 16)

    ASCR = ASCR0
    w32 = av(ASCR, 128, 1024, F32)
    wbf0 = av(ASCR + 2048, 128, 1024)
    wtT0 = av(ASCR + 3072, 128, 1024)

    for (Wsrc, Wdst) in ((Wq, wT_q), (Wk, wT_k), (Wv, wT_v), (Wo, wT_o)):
        for r in range(8):
            @block.sync
            def _(sy, Wsrc=Wsrc, r=r):
                w(sy, "s_dve")
                inc(sy.dma_start(out=w32, in_=Wsrc[128 * r:128 * (r + 1), :]), "s_ld", 16)

            @block.vector
            def _(v):
                w(v, "s_ld")
                w(v, "s_pe")
                inc(v.tensor_copy(wbf0, w32), "s_dve")

            @block.tensor
            def _(t):
                w(t, "s_dve")
                for cq in range(8):
                    inc(t.transpose(trps[:, 128 * cq:128 * (cq + 1)],
                                    wbf0[:, 128 * cq:128 * (cq + 1)], ident_bf[:]), "s_pe")

            @block.vector
            def _(v):
                w(v, "s_pe")
                w(v, "s_sc")
                inc(v.tensor_copy(wtT0, trps), "s_dve")

            @block.scalar
            def _(s, Wdst=Wdst, r=r):
                w(s, "s_dve")
                inc(s.dma_start(
                    out=Wdst.rearrange("(cq p) o -> p cq o", p=128)[:, :, 128 * r:128 * (r + 1)],
                    in_=wtT0.rearrange("p (cq o) -> p cq o", o=128)), "s_sc", 16)

    rw32 = av(ASCR + 4096, 8, 1024, F32)
    rwbf = av(ASCR + 6144, 8, 1024)

    @block.sync
    def _(sy):
        w(sy, "s_dve")
        inc(sy.dma_start(out=rw32, in_=rw), "s_ld", 16)

    @block.vector
    def _(v):
        w(v, "s_ld")
        w(v, "s_pe")
        inc(v.tensor_copy(rwbf, rw32), "s_dve")

    for ch in range(8):
        @block.tensor
        def _(t, ch=ch):
            w(t, "s_dve")
            inc(t.transpose(ps[:, 3072:3076].bitcast(BF16), rwbf[:, 128 * ch:128 * (ch + 1)],
                            ident_bf[0:8, 0:8]), "s_pe")

        @block.vector
        def _(v, ch=ch):
            w(v, "s_pe")
            inc(v.tensor_copy(rwT[:, 8 * ch:8 * (ch + 1)],
                              ps[:, 3072:3076].bitcast(BF16)[:, 0:8]), "s_dve")

    @block.scalar
    def _(s):
        w(s, "s_sc")
    @block.gpsimd
    def _(g):
        w(g, "s_g")
    barrier()

    @block.sync
    def _(sy):
        sy.reg_mov(reg(sy, "m_pef"), 0)
    @block.tensor
    def _(t):
        t.reg_mov(reg(t, "m_ldf"), 0)
        t.reg_mov(reg(t, "m_actf"), 0)
        t.reg_mov(reg(t, "m_dvef"), 0)
        t.reg_mov(reg(t, "m_dvef2"), 0)
    @block.scalar
    def _(s):
        s.reg_mov(reg(s, "m_pef"), 0)
        s.reg_mov(reg(s, "m_dvef2"), 0)
        s.reg_mov(reg(s, "m_scf"), 0)
    @block.vector
    def _(v):
        v.reg_mov(reg(v, "m_pef"), 0)
        v.reg_mov(reg(v, "m_scf"), 0)
    @block.gpsimd
    def _(g):
        g.reg_mov(reg(g, "m_pef"), 0)
        g.reg_mov(reg(g, "m_pef2"), 0)
        g.reg_mov(reg(g, "m_scf"), 0)

    def emit_ln(x_src, with_logits, ln2T=None):
        xa = [fsv(0, 128, 1024), fsv(1024, 128, 1024)]
        xsq = fsv(2048, 128, 1024)
        stat = fsv(3072, 128, 16)
        xhat = fsv(3088, 128, 1024)
        hiT = av(FS + 2 * 4112, 128, 1024)
        lgsb = fsv(4624, 128, 256)
        lgps = ps[:, 3584:3592]

        for i in range(NTILE):
            pb = i % 2

            lsem = "s_lda" if pb == 0 else "s_ldb"

            @block.sync
            def _(sy, i=i, pb=pb, lsem=lsem):
                if i >= 2:
                    w(sy, "s_act")
                inc(sy.dma_start(out=xa[pb], in_=x_src[128 * i:128 * (i + 1), :]), lsem, 16)

            @block.scalar
            def _(s, pb=pb, lsem=lsem):
                w(s, lsem)
                w(s, "s_dve")
                inc(s.activation(xsq, xa[pb], AF.Square), "s_act")

            @block.vector
            def _(v, pb=pb, lsem=lsem):
                w(v, lsem)
                inc(v.tensor_reduce(stat[:, 0:1], xa[pb], AXX, ALU.add, negate=True), "s_dve")
                w(v, "s_act")
                inc(v.tensor_reduce(stat[:, 1:2], xsq, AXX, ALU.add), "s_dve")
                w(v, "s_dve")
                inc(v.tensor_scalar(stat[:, 2:3], stat[:, 0:1], 1.0 / D, None, ALU.mult), "s_dve")
                w(v, "s_dve")
                inc(v.tensor_scalar(stat[:, 3:4], stat[:, 1:2], 1.0 / D, None, ALU.mult), "s_dve")
                w(v, "s_dve")
                inc(v.tensor_tensor(stat[:, 4:5], stat[:, 2:3], stat[:, 2:3], ALU.mult), "s_dve")
                w(v, "s_dve")
                inc(v.tensor_scalar(stat[:, 5:6], stat[:, 3:4], stat[:, 4:5], 1e-5,
                                    ALU.subtract, ALU.add), "s_dve")

            @block.scalar
            def _(s):
                w(s, "s_dve")
                inc(s.activation(stat[:, 6:7], stat[:, 5:6], AF.Sqrt), "s_act")

            @block.vector
            def _(v):
                w(v, "s_act")
                inc(v.reciprocal(stat[:, 7:8], stat[:, 6:7]), "s_dve")
                w(v, "s_dve")
                inc(v.tensor_tensor(stat[:, 8:9], stat[:, 2:3], stat[:, 7:8], ALU.mult), "s_dve")

            @block.scalar
            def _(s, pb=pb):
                w(s, "s_dve")
                inc(s.activation(xhat, xa[pb], AF.Identity, bias=stat[:, 8:9], scale=stat[:, 7:8]), "s_act")

            @block.vector
            def _(v, pb=pb):
                w(v, "s_act")
                if with_logits:
                    w(v, "s_sc")
                else:
                    w(v, "s_pe")
                inc(v.tensor_copy(lnrot[pb], xhat), "s_dve")

            if with_logits:
                @block.scalar
                def _(s, i=i, pb=pb):
                    w(s, "s_dve")
                    inc(s.dma_start(out=ln_dram[128 * i:128 * (i + 1), :], in_=lnrot[pb]), "s_sc", 16)

            @block.tensor
            def _(t, pb=pb):
                w(t, "s_dve")
                for cq in range(8):
                    inc(t.transpose(trps[:, 128 * cq:128 * (cq + 1)],
                                    lnrot[pb][:, 128 * cq:128 * (cq + 1)], ident_bf[:]), "s_pe")

            if with_logits:
                @block.vector
                def _(v):
                    w(v, "s_pe")
                    inc(v.tensor_copy(hiT, trps), "s_dve")

                @block.tensor
                def _(t):
                    w(t, "s_dve")
                    for dc in range(8):
                        inc(t.matmul(lgps, hiT[:, 128 * dc:128 * (dc + 1)],
                                     rwT[:, 8 * dc:8 * (dc + 1)],
                                     start=(dc == 0), stop=(dc == 7)), "s_pe")

                @block.vector
                def _(v, i=i):
                    w(v, "s_pe")
                    inc(v.tensor_copy(lgsb[:, 8 * i:8 * (i + 1)], lgps), "s_dve")
            else:
                @block.vector
                def _(v, i=i):
                    w(v, "s_pe")
                    inc(v.tensor_copy(
                        ln2T[:, 1024 * i:1024 * (i + 1)].rearrange("p (cq o) -> p cq o", o=128),
                        trps.rearrange("p (cq o) -> p cq o", o=128)), "s_dve")
        return lgsb

    def emit_ffn(app, x_src, y_dst, lgi, lgo):
        _sid = scope(f"f{app}_lnlog")
        lgsb = emit_ln(x_src, with_logits=True)

        @block.scalar
        def _(s):
            w(s, "s_dve")
            inc(s.dma_start(out=lgi.rearrange("(i p) e -> p i e", p=128),
                            in_=lgsb.rearrange("p (i e) -> p i e", e=8)), "s_sc", 16)

        @block.gpsimd
        def _(g):
            w(g, "s_sc")
            inc(g.collective_compute("AllGather", ALU.bypass,
                                     replica_groups=[list(range(NCORES))],
                                     ins=[lgi], outs=[lgo]), "s_cc", 1)

        barrier()

        escope(f"f{app}_lnlog", _sid)
        _sid = scope(f"f{app}_route")
        lgT = fsv(0, 128, 2048)
        mx = fsv(2048, 128, 256)
        eqh = fsv(2304, 128, 2048)
        t2 = fsv(4352, 128, 2048)
        e_tm = fsv(6400, 128, 256)
        ex = fsv(6656, 128, 2048)
        sume = fsv(8704, 128, 256)
        p_tm = fsv(8960, 128, 256)
        eT = [fsv(9216, 128, 128), fsv(9344, 128, 128)]
        oh = [fsv(9472, 128, 1024), fsv(10496, 128, 1024)]
        incl = [fsv(11520, 128, 1024), fsv(12544, 128, 1024)]
        Tj = fsv(13568, 128, 16)
        Tjbf = av(FS + 2 * 13584, 128, 16)
        Bj = fsv(13600, 128, 16)
        tot = fsv(13632, 1, 24)
        totb = fsv(13664, 128, 8)
        posf = fsv(13696, 128, 1024)
        pos_j = fsv(14720, 128, 256)
        scr = fsv(14976, 128, 1024)

        @block.sync
        def _(sy):
            w(sy, "s_cc")
            inc(sy.dma_start(out=lgT.rearrange("p (i e) -> p i e", e=8),
                             in_=lgo.rearrange("(i p) e -> p i e", p=128)), "s_ld", 16)

        @block.gpsimd
        def _(g):
            inc(g.iota(iota8[:], pattern=[[0, 256], [1, 8]], base=0, channel_multiplier=0,
                       allow_small_or_imprecise_dtypes=True), "s_gp")

        @block.vector
        def _(v):
            w(v, "s_gp")
            w(v, "s_sc")
            inc(v.tensor_scalar(iota8m8[:], iota8[:], -8.0, None, ALU.add), "s_dve")
            w(v, "s_ld")
            inc(v.tensor_reduce(mx, lgT.rearrange("p (i e) -> p i e", e=8), AXX, ALU.max), "s_dve")
            w(v, "s_dve")
            inc(v.tensor_tensor(eqh.rearrange("p (i e) -> p i e", e=8),
                                lgT.rearrange("p (i e) -> p i e", e=8),
                                mx.unsqueeze(-1).broadcast_to((128, 256, 8)),
                                ALU.is_equal), "s_dve")
            w(v, "s_dve")
            inc(v.tensor_tensor(t2, eqh, iota8m8[:], ALU.mult), "s_dve")
            w(v, "s_dve")
            inc(v.tensor_scalar(t2, t2, 8.0, None, ALU.add), "s_dve")
            w(v, "s_dve")
            inc(v.tensor_reduce(e_tm, t2.rearrange("p (i e) -> p i e", e=8), AXX, ALU.min), "s_dve")
            w(v, "s_dve")
            inc(v.tensor_tensor(eqh.rearrange("p (i e) -> p i e", e=8),
                                lgT.rearrange("p (i e) -> p i e", e=8),
                                mx.unsqueeze(-1).broadcast_to((128, 256, 8)),
                                ALU.subtract), "s_dve")

        @block.scalar
        def _(s):
            w(s, "s_dve")
            inc(s.activation(ex, eqh, AF.Exp), "s_act")

        @block.vector
        def _(v):
            w(v, "s_act")
            inc(v.tensor_reduce(sume, ex.rearrange("p (i e) -> p i e", e=8), AXX, ALU.add), "s_dve")
            w(v, "s_dve")
            inc(v.reciprocal(p_tm, sume), "s_dve")

        for j in range(2):
            @block.tensor
            def _(t, j=j):
                w(t, "s_dve")
                inc(t.transpose(ps[:, 0:128], e_tm[:, 128 * j:128 * (j + 1)], ident_f[:]), "s_pe")

            @block.vector
            def _(v, j=j):
                w(v, "s_pe")
                inc(v.tensor_copy(eT[j], ps[:, 0:128]), "s_dve")

        for j in range(2):
            @block.vector
            def _(v, j=j):
                w(v, "s_dve")
                inc(v.tensor_tensor(oh[j].rearrange("p (t e) -> p t e", e=8),
                                    iota8[:, 0:1024].rearrange("p (t e) -> p t e", e=8),
                                    eT[j].unsqueeze(-1).broadcast_to((128, 128, 8)),
                                    ALU.is_equal), "s_dve")
                for e in range(E):
                    w(v, "s_dve")
                    inc(v.tensor_tensor_scan(
                        incl[j].rearrange("p (t e) -> p e t", e=8)[:, e, :],
                        oh[j].rearrange("p (t e) -> p e t", e=8)[:, e, :],
                        zeros128[:], 0.0, ALU.add, ALU.add), "s_dve")
                w(v, "s_dve")
                inc(v.tensor_copy(Tj[:, 8 * j:8 * (j + 1)], incl[j][:, 1016:1024]), "s_dve")
                w(v, "s_dve")
                inc(v.tensor_copy(Tjbf[:, 8 * j:8 * (j + 1)], Tj[:, 8 * j:8 * (j + 1)]), "s_dve")

        @block.tensor
        def _(t):
            w(t, "s_dve")
            inc(t.matmul(ps[:, 0:8], lt_bf[:], Tjbf[:, 0:8], start=True, stop=True), "s_pe")
            inc(t.matmul(ps[:, 512:520], lt_bf[:], Tjbf[:, 8:16], start=True, stop=True), "s_pe")
            inc(t.matmul(ps[0:1, 1024:1032], ones_col_bf[:], Tjbf[:, 0:8], start=True, stop=True), "s_pe")
            inc(t.matmul(ps[0:1, 1536:1544], ones_col_bf[:], Tjbf[:, 8:16], start=True, stop=True), "s_pe")

        @block.vector
        def _(v):
            w(v, "s_pe")
            inc(v.tensor_copy(Bj[:, 0:8], ps[:, 0:8]), "s_dve")
            inc(v.tensor_copy(Bj[:, 8:16], ps[:, 512:520]), "s_dve")
            inc(v.tensor_copy(tot[0:1, 0:8], ps[0:1, 1024:1032]), "s_dve")
            inc(v.tensor_copy(tot[0:1, 8:16], ps[0:1, 1536:1544]), "s_dve")
            w(v, "s_dve")
            inc(v.tensor_tensor(tot[0:1, 16:24], tot[0:1, 0:8], tot[0:1, 8:16], ALU.add), "s_dve")

        @block.tensor
        def _(t):
            w(t, "s_dve")
            inc(t.matmul(ps[:, 2048:2056], onesf_row[:], tot[0:1, 0:8], start=True, stop=True), "s_pe")

        @block.vector
        def _(v):
            w(v, "s_pe")
            inc(v.tensor_copy(totb[:], ps[:, 2048:2056]), "s_dve")
            w(v, "s_dve")
            inc(v.tensor_tensor(Bj[:, 8:16], Bj[:, 8:16], totb[:], ALU.add), "s_dve")

        for j in range(2):
            @block.vector
            def _(v, j=j):
                w(v, "s_dve")
                w(v, "s_sc")
                inc(v.tensor_tensor(posf, incl[j], oh[j], ALU.subtract), "s_dve")
                w(v, "s_dve")
                inc(v.tensor_tensor(posf.rearrange("p (t e) -> p t e", e=8),
                                    posf.rearrange("p (t e) -> p t e", e=8),
                                    Bj[:, 8 * j:8 * (j + 1)].unsqueeze(1).broadcast_to((128, 128, 8)),
                                    ALU.add), "s_dve")
                w(v, "s_dve")
                inc(v.tensor_tensor(scr, posf, oh[j], ALU.mult), "s_dve")
                w(v, "s_dve")
                inc(v.tensor_reduce(pos_j[:, 128 * j:128 * (j + 1)],
                                    scr.rearrange("p (t e) -> p t e", e=8), AXX, ALU.add), "s_dve")

            @block.scalar
            def _(s, j=j):
                w(s, "s_dve")
                inc(s.dma_start(out=pos_dram[128 * j:128 * (j + 1), :],
                                in_=pos_j[:, 128 * j:128 * (j + 1)]), "s_sc", 16)
                inc(s.dma_start(out=e_dram[128 * j:128 * (j + 1), :], in_=eT[j]), "s_sc", 16)

        @block.scalar
        def _(s):
            w(s, "s_dve")
            inc(s.dma_start(out=p_dram, in_=p_tm), "s_sc", 16)
            inc(s.dma_start(out=B_dram[0:128, :], in_=Bj[:, 0:8]), "s_sc", 16)
            inc(s.dma_start(out=B_dram[128:256, :], in_=Bj[:, 8:16]), "s_sc", 16)
            inc(s.dma_start(out=B_dram[256:257, :], in_=tot[0:1, 16:24]), "s_sc", 16)

        barrier()

        posbt = fsv(0, 32, 128)
        ebt = fsv(128, 32, 128)
        e4bt = fsv(256, 32, 128)
        pown = fsv(384, 128, 32)
        B32 = fsv(416, 1, 16)
        B32b = fsv(432, 32, 8)
        ohown = fsv(440, 32, 1024)
        bsel = fsv(1464, 32, 128)
        scatbt = fsv(1592, 32, 128)
        gathbt = fsv(1720, 32, 128)
        posG = fsv(1848, 128, 32)
        scat_f = fsv(1880, 128, 32)
        gath_f = fsv(1912, 128, 32)
        kept = fsv(1944, 128, 32)
        oscr = fsv(1976, 32, 1024)

        @block.sync
        def _(sy):
            r1 = reg(sy, "o1")
            r2 = reg(sy, "o2")
            sy.reg_load(r1, cc_i[0:1, 0:1])
            sy.reg_mul(r2, r1, 32 * 128)
            inc(sy.dma_start(out=posbt, in_=bass.AP(pos_dram.tensor, r2, [[128, 32], [1, 128]])), "s_ld", 16)
            _free_dyn_regs(nc, "SP")
            inc(sy.dma_start(out=ebt, in_=bass.AP(e_dram.tensor, r2, [[128, 32], [1, 128]])), "s_ld", 16)
            _free_dyn_regs(nc, "SP")
            sy.reg_mul(r2, r1, 32)
            inc(sy.dma_start(out=pown, in_=bass.AP(p_dram.tensor, r2, [[256, 128], [1, 32]])), "s_ld", 16)
            _free_dyn_regs(nc, "SP")
            sy.reg_mul(r2, r1, 32 * 8)
            inc(sy.dma_start(out=B32[0:1, 0:8], in_=bass.AP(B_dram.tensor, r2, [[8, 1], [1, 8]])), "s_ld", 16)
            _free_dyn_regs(nc, "SP")
            sy.reg_add(r2, r2, 32 * 8)
            inc(sy.dma_start(out=B32[0:1, 8:16], in_=bass.AP(B_dram.tensor, r2, [[8, 1], [1, 8]])), "s_ld", 16)
            _free_dyn_regs(nc, "SP")

        @block.tensor
        def _(t):
            w(t, "s_ld")
            inc(t.matmul(ps[0:32, 2048:2056], onesf_row[0:1, 0:32], B32[0:1, 0:8], start=True, stop=True), "s_pe")

        @block.vector
        def _(v):
            w(v, "s_pe")
            inc(v.tensor_copy(B32b[:], ps[0:32, 2048:2056]), "s_dve")
            w(v, "s_dve")
            inc(v.tensor_tensor(ohown.rearrange("p (t e) -> p t e", e=8),
                                iota8[0:32, 0:1024].rearrange("p (t e) -> p t e", e=8),
                                ebt.unsqueeze(-1).broadcast_to((32, 128, 8)),
                                ALU.is_equal), "s_dve")
            w(v, "s_dve")
            inc(v.tensor_tensor(oscr.rearrange("p (t e) -> p t e", e=8),
                                ohown.rearrange("p (t e) -> p t e", e=8),
                                B32b[0:32, :].unsqueeze(1).broadcast_to((32, 128, 8)),
                                ALU.mult), "s_dve")
            w(v, "s_dve")
            inc(v.tensor_reduce(bsel, oscr.rearrange("p (t e) -> p t e", e=8), AXX, ALU.add), "s_dve")
            w(v, "s_dve")
            inc(v.tensor_tensor(scatbt, posbt, bsel, ALU.subtract), "s_dve")
            inc(v.tensor_scalar(e4bt, ebt, float(SECT), None, ALU.mult), "s_dve")
            w(v, "s_dve")
            inc(v.tensor_tensor(scatbt, scatbt, e4bt, ALU.add), "s_dve")
            inc(v.tensor_scalar(gathbt, posbt, float(CAP), None, ALU.is_ge), "s_dve")
            w(v, "s_dve")
            inc(v.tensor_scalar(gathbt, gathbt, BIG, None, ALU.mult), "s_dve")
            w(v, "s_dve")
            inc(v.tensor_tensor(gathbt, gathbt, scatbt, ALU.add), "s_dve")
            inc(v.tensor_tensor(cnts_f[:], B32[0:1, 8:16], B32[0:1, 0:8], ALU.subtract), "s_dve")
            w(v, "s_dve")
            inc(v.tensor_copy(cnts_i[:], cnts_f[:]), "s_dve")

        @block.tensor
        def _(t):
            w(t, "s_dve")
            inc(t.transpose(ps[:, 0:32], scatbt, ident_f[0:32, 0:32]), "s_pe")
            inc(t.transpose(ps[:, 512:544], gathbt, ident_f[0:32, 0:32]), "s_pe")
            inc(t.transpose(ps[:, 1024:1056], posbt, ident_f[0:32, 0:32]), "s_pe")

        @block.vector
        def _(v):
            w(v, "s_pe")
            inc(v.tensor_copy(scat_f, ps[:, 0:32]), "s_dve")
            inc(v.tensor_copy(gath_f, ps[:, 512:544]), "s_dve")
            inc(v.tensor_copy(posG, ps[:, 1024:1056]), "s_dve")
            w(v, "s_dve")
            inc(v.tensor_copy(scat_u[:], scat_f), "s_dve")
            inc(v.tensor_copy(gath_u[:], gath_f), "s_dve")
            inc(v.tensor_scalar(kept, posG, float(CAP), None, ALU.is_lt), "s_dve")
            w(v, "s_dve")
            inc(v.tensor_tensor(gsc_sb[:], kept, pown, ALU.mult), "s_dve")
            w(v, "s_dve")
            inc(v.tensor_scalar(gsc_sb[:], gsc_sb[:], 0.5, None, ALU.mult), "s_dve")

        barrier()

        escope(f"f{app}_route", _sid)
        _sid = scope(f"f{app}_scat")
        for i in range(NTILE):
            pb = i % 2

            lsem = "s_lda" if pb == 0 else "s_ldb"

            @block.sync
            def _(sy, i=i, pb=pb, lsem=lsem):
                if i >= 2:
                    w(sy, "s_g")
                inc(sy.dma_start(out=lnrot[pb], in_=ln_dram[128 * i:128 * (i + 1), :]), lsem, 16)

            @block.gpsimd
            def _(g, i=i, pb=pb, lsem=lsem):
                w(g, lsem)
                inc(g.indirect_dma_start(
                    out=xg_dram, out_offset=IndirectOffsetOnAxis(scat_u[:, i:i + 1], 0),
                    in_=lnrot[pb], in_offset=None), "s_g", 16)

        barrier()

        escope(f"f{app}_scat", _sid)
        _sid = scope(f"f{app}_exp")
        w1sb = av(0, 128, 32768)
        w2sb = av(32768, 128, 32768)
        xgT = av(65536, 128, 2048)
        xgtok = av(67584, 128, 2048)
        hb = [av(69632, 128, 256), av(69888, 128, 256)]
        eosb = av(70144, 128, 2048)
        b1cur = [av(73216, 1, FF), av(78336, 1, FF)]
        b2cur = [av(77312, 1, D), av(82432, 1, D)]
        hps = [ps[:, 0:256], ps[:, 512:768]]
        ops_ = [ps[:, 1024:2048], ps[:, 2048:3072]]
        trp2 = ps[:, 3072:3584].bitcast(BF16)

        for e in range(E):
            w1_base = cnt["s_w1"]
            w2_base = cnt["s_w2"]

            @block.gpsimd
            def _(g, e=e, app=app):
                mp = reg(g, "m_pef")
                mp2 = reg(g, "m_pef2")
                t2 = reg(g, "t2")
                gt = reg(g, "gt")
                first = (app == 0 and e == 0)
                if not first:
                    g.reg_add(t2, mp, -PE_T)
                if not first:
                    g.wait_ge(sems["s_pef"], mp2)
                w(g, "s_w1", cnt["s_w1"])
                inc(g.dma_start(out=b1cur[e % 2], in_=b1a[0:1, FF * e:FF * (e + 1)]), "s_w1", 16)
                w(g, "s_w2", cnt["s_w2"])
                inc(g.dma_start(out=b2cur[e % 2], in_=b2a[0:1, D * e:D * (e + 1)]), "s_w2", 16)
                for fc in range(32):
                    if not first:
                        g.reg_add(gt, t2, 3 if fc == 0 else 2 + 2 * fc)
                        g.wait_ge(sems["s_pef"], gt)
                    w(g, "s_w1", cnt["s_w1"])
                    inc(g.dma_start(
                        out=w1sb.rearrange("p (dc f) -> p dc f", f=FF)[:, :, 128 * fc:128 * (fc + 1)],
                        in_=w1a[e].rearrange("(dc p) f -> p dc f", p=128)[:, :, 128 * fc:128 * (fc + 1)]),
                        "s_w1", 16)
                    if not first:
                        g.reg_add(gt, t2, 2 * fc + 5)
                        g.wait_ge(sems["s_pef"], gt)
                    w(g, "s_w2", cnt["s_w2"])
                    inc(g.dma_start(out=w2sb[:, 1024 * fc:1024 * (fc + 1)],
                                    in_=w2a[e][128 * fc:128 * (fc + 1), :]), "s_w2", 16)
                g.reg_mov(mp2, mp)
                n = reg(g, "nt")
                g.reg_load(n, cnts_i[0:1, e:e + 1])
                g.reg_add(n, n, ETILE - 1)
                g.reg_div(n, n, ETILE)
                t1 = reg(g, "t1")
                g.reg_mul(t1, n, PE_T)
                g.reg_add(mp, mp, t1)
                ms = reg(g, "m_scf")
                g.reg_mul(t1, n, SCF_T)
                g.reg_add(ms, ms, t1)

            @block.sync
            def _(sy, e=e):
                n = reg(sy, "nt")
                sy.reg_load(n, cnts_i[0:1, e:e + 1])
                sy.reg_add(n, n, ETILE - 1)
                sy.reg_div(n, n, ETILE)
                off = reg(sy, "off")
                sy.reg_mov(off, e * SECT * D)
                mp = reg(sy, "m_pef")
                with _fori(nc, sy, "SP", 0, n):
                    sy.wait_ge(sems["s_pef"], mp)
                    sy.dma_start(out=xgtok[:].rearrange("p (a d) -> p a d", d=D),
                                 in_=bass.AP(xg_dram.tensor, off, [[D, 128], [128 * D, 2], [1, D]])
                                 ).then_inc(sems["s_ldf"], 16)
                    _free_dyn_regs(nc, "SP")
                    sy.reg_add(off, off, 256 * D)
                    sy.reg_add(mp, mp, PE_T)

            @block.tensor
            def _(t, e=e, w1_base=w1_base, w2_base=w2_base):
                n = reg(t, "nt")
                t.reg_load(n, cnts_i[0:1, e:e + 1])
                t.reg_add(n, n, ETILE - 1)
                t.reg_div(n, n, ETILE)
                m_ldf = reg(t, "m_ldf")
                m_actf = reg(t, "m_actf")
                m_dvef = reg(t, "m_dvef")
                m_dvef2 = reg(t, "m_dvef2")
                with _fori(nc, t, "PE", 0, n):
                    t.reg_add(m_ldf, m_ldf, LDF_T)
                    t.wait_ge(sems["s_ldf"], m_ldf)
                    for sblk in range(2):
                        t.wait_ge(sems["s_dvef"], m_dvef)
                        for dc in range(8):
                            tr = t.transpose(trp2[:, 128 * dc:128 * (dc + 1)],
                                             xgtok[:, 1024 * sblk + 128 * dc:1024 * sblk + 128 * (dc + 1)],
                                             ident_bf[:])
                            if dc == 7:
                                tr.then_inc(sems["s_pef"], 1)
                        t.reg_add(m_dvef, m_dvef, 1)
                    t.wait_ge(sems["s_dvef"], m_dvef)
                    t.wait_ge(sems["s_dvef2"], m_dvef2)
                    t.reg_add(m_dvef2, m_dvef2, 1)
                    for f in range(33):
                        if f <= 31:
                            fp = f % 2
                            t.wait_ge(sems["s_w1"], w1_base + 16 * (f + 2))
                            for dc in range(8):
                                t.matmul(hps[fp],
                                         w1sb.rearrange("p (dc f) -> p dc f", f=FF)[:, dc, 128 * f:128 * (f + 1)],
                                         xgT[:, 256 * dc:256 * (dc + 1)],
                                         start=(dc == 0), stop=False)
                            mm = t.matmul(hps[fp], b1cur[e % 2][0:1, 128 * f:128 * (f + 1)],
                                          ones_row_bf[0:1, 0:256], start=False, stop=True)
                            mm.then_inc(sems["s_pef"], 1)
                        if f >= 1:
                            g = f - 1
                            gp = g % 2
                            t.reg_add(m_actf, m_actf, 1)
                            t.wait_ge(sems["s_actf"], m_actf)
                            t.wait_ge(sems["s_w2"], w2_base + 16 * (g + 2))
                            if g == 0:
                                for sub in range(2):
                                    for dh in range(2):
                                        t.matmul(ops_[sub][:, 512 * dh:512 * (dh + 1)],
                                                 ones_row_bf[0:1, 0:128],
                                                 b2cur[e % 2][0:1, 512 * dh:512 * (dh + 1)],
                                                 start=True, stop=False)
                            mm2 = None
                            for sub in range(2):
                                for dh in range(2):
                                    mm2 = t.matmul(ops_[sub][:, 512 * dh:512 * (dh + 1)],
                                                   hb[gp][:, 128 * sub:128 * (sub + 1)],
                                                   w2sb[:, 1024 * g + 512 * dh:1024 * g + 512 * (dh + 1)],
                                                   start=False, stop=(g == 31))
                            mm2.then_inc(sems["s_pef"], 1)

            @block.scalar
            def _(s, e=e):
                n = reg(s, "nt")
                s.reg_load(n, cnts_i[0:1, e:e + 1])
                s.reg_add(n, n, ETILE - 1)
                s.reg_div(n, n, ETILE)
                m_pef = reg(s, "m_pef")
                m_dvef2 = reg(s, "m_dvef2")
                m_scf = reg(s, "m_scf")
                off = reg(s, "off")
                s.reg_mov(off, e * SECT * D)
                with _fori(nc, s, "Activation", 0, n):
                    s.reg_add(m_pef, m_pef, 2)
                    for f in range(32):
                        fp = f % 2
                        s.reg_add(m_pef, m_pef, 1 if f <= 1 else 2)
                        s.wait_ge(sems["s_pef"], m_pef)
                        s.activation(hb[fp], hps[fp], AF.Gelu).then_inc(sems["s_actf"], 1)
                    s.reg_add(m_pef, m_pef, 2)
                    s.reg_add(m_dvef2, m_dvef2, 1)
                    s.wait_ge(sems["s_dvef2"], m_dvef2)
                    s.dma_start(out=bass.AP(eo_dram.tensor, off, [[D, 128], [128 * D, 2], [1, D]]),
                                in_=eosb.rearrange("p (a d) -> p a d", d=D)
                                ).then_inc(sems["s_scf"], 16)
                    _free_dyn_regs(nc, "Activation")
                    s.reg_add(off, off, ETILE * D)
                    s.reg_add(m_scf, m_scf, 16)

            @block.vector
            def _(v, e=e):
                n = reg(v, "nt")
                v.reg_load(n, cnts_i[0:1, e:e + 1])
                v.reg_add(n, n, ETILE - 1)
                v.reg_div(n, n, ETILE)
                m_pef = reg(v, "m_pef")
                m_scf = reg(v, "m_scf")
                with _fori(nc, v, "DVE", 0, n):
                    for sblk in range(2):
                        v.reg_add(m_pef, m_pef, 1)
                        v.wait_ge(sems["s_pef"], m_pef)
                        v.tensor_copy(
                            xgT.rearrange("p (dc sl) -> p dc sl", sl=256)[:, :, 128 * sblk:128 * (sblk + 1)],
                            trp2.rearrange("p (dc sl) -> p dc sl", sl=128)
                            ).then_inc(sems["s_dvef"], 1)
                    v.reg_add(m_pef, m_pef, 64)
                    v.wait_ge(sems["s_pef"], m_pef)
                    v.wait_ge(sems["s_scf"], m_scf)
                    v.tensor_copy(eosb[:, 0:1024], ops_[0])
                    v.tensor_copy(eosb[:, 1024:2048], ops_[1]).then_inc(sems["s_dvef2"], 1)
                    v.reg_add(m_scf, m_scf, 16)

        @block.scalar
        def _(s):
            s.wait_ge(sems["s_scf"], reg(s, "m_scf"))
        @block.gpsimd
        def _(g):
            w(g, "s_g")
        barrier()

        escope(f"f{app}_exp", _sid)
        _sid = scope(f"f{app}_yadd")
        ya = [av(71168 + 1024 * k, 128, 1024) for k in (0, 1)]
        xa2 = [fsv(0, 128, 1024), fsv(1024, 128, 1024)]
        xn = [fsv(2048, 128, 1024), fsv(3072, 128, 1024)]

        for i in range(NTILE):
            pb = i % 2

            gsem = "s_g" if pb == 0 else "s_g2"
            lsem = "s_lda" if pb == 0 else "s_ldb"

            @block.gpsimd
            def _(g, i=i, pb=pb, gsem=gsem):
                if i >= 2:
                    w(g, "s_dve")
                inc(g.memset(ya[pb], 0.0), "s_gp")
                w(g, "s_gp")
                inc(g.indirect_dma_start(
                    out=ya[pb], out_offset=None,
                    in_=eo_dram, in_offset=IndirectOffsetOnAxis(gath_u[:, i:i + 1], 0),
                    bounds_check=reg(g, "bchk"), oob_is_err=False), gsem, 16)

            @block.sync
            def _(sy, i=i, pb=pb, lsem=lsem):
                if i >= 2:
                    w(sy, "s_dve")
                inc(sy.dma_start(out=xa2[pb], in_=x_src[128 * i:128 * (i + 1), :]), lsem, 16)

            @block.vector
            def _(v, i=i, pb=pb, gsem=gsem, lsem=lsem):
                w(v, gsem)
                w(v, lsem)
                if i >= 2:
                    w(v, "s_sc")
                inc(v.tensor_scalar(xn[pb], ya[pb], gsc_sb[:, i:i + 1], None, ALU.mult), "s_dve")
                w(v, "s_dve")
                inc(v.tensor_tensor(xn[pb], xn[pb], xa2[pb], ALU.add), "s_dve")

            @block.scalar
            def _(s, i=i, pb=pb):
                w(s, "s_dve")
                inc(s.dma_start(out=y_dst[128 * i:128 * (i + 1), :], in_=xn[pb]), "s_sc", 16)

        @block.scalar
        def _(s):
            w(s, "s_sc")
        barrier()
        escope(f"f{app}_yadd", _sid)

    def emit_attention(x_src, x_dst):
        _sid = scope("a_ln")
        ln2T = av(0, 128, 32768)
        emit_ln(x_src, with_logits=False, ln2T=ln2T)
        barrier()
        escope("a_ln", _sid)
        _sid = scope("a_kvq")

        wTsb = av(32768, 128, 8192)
        qstage = av(57344, 128, 4096)
        bv_row = av(82176, 1, D)
        bo_row = av(83200, 1, D)

        @block.gpsimd
        def _(g):
            inc(g.dma_start(out=bv_row, in_=bv), "s_g", 16)
            inc(g.dma_start(out=bo_row, in_=bo), "s_g", 16)

        def ln2T_rhs(dc, t0, ntok):
            i0 = t0 // 128
            ni = ntok // 128
            return ln2T.rearrange("p (i cq o) -> p i cq o", cq=8, o=128)[:, i0:i0 + ni, dc, :]

        @block.sync
        def _(sy):
            w(sy, "s_dve")
            inc(sy.dma_start(out=wTsb.rearrange("p (dc o) -> p dc o", o=D),
                             in_=wT_k.rearrange("(dc p) o -> p dc o", p=128)), "s_ld", 16)

        dve0 = cnt["s_dve"]
        for ch in range(8):
            for ts in range(8):
                bank = (ch * 8 + ts) % 2
                pcol = 1024 * bank

                @block.tensor
                def _(t, ch=ch, ts=ts, pcol=pcol):
                    w(t, "s_ld")
                    if ch * 8 + ts >= 2:
                        w(t, "s_dve", dve0 + ch * 8 + ts - 1)
                    for dc in range(8):
                        inc(t.matmul(ps[:, pcol:pcol + 512],
                                     wTsb[:, 1024 * dc + 128 * ch:1024 * dc + 128 * (ch + 1)],
                                     ln2T_rhs(dc, 512 * ts, 512),
                                     start=(dc == 0), stop=(dc == 7)), "s_pe")

                @block.vector
                def _(v, ch=ch, ts=ts, pcol=pcol):
                    w(v, "s_pe")
                    if ts == 0:
                        w(v, "s_sc")
                    inc(v.tensor_scalar(qstage[:, 512 * ts:512 * (ts + 1)],
                                        ps[:, pcol:pcol + 512], bk_sb[:, ch:ch + 1], None,
                                        ALU.add), "s_dve")

            @block.scalar
            def _(s, ch=ch):
                w(s, "s_dve")
                inc(s.dma_start(out=kt_in[128 * ch:128 * (ch + 1), :], in_=qstage), "s_sc", 16)

        @block.gpsimd
        def _(g):
            w(g, "s_sc")
            inc(g.collective_compute("AllGather", ALU.bypass,
                                     replica_groups=[[0, 1], [2, 3], [4, 5], [6, 7]],
                                     ins=[kt_in], outs=[kt_out]), "s_cc", 1)

        @block.sync
        def _(sy):
            w(sy, "s_pe")
            inc(sy.dma_start(out=wTsb.rearrange("p (dc o) -> p dc o", o=D),
                             in_=wT_v.rearrange("(dc p) o -> p dc o", p=128)), "s_ld", 16)

        dve0v = cnt["s_dve"]
        for ts in range(NTILE):
            for dh in range(2):
                k = ts * 2 + dh
                bank = k % 2
                pcol = 1024 * bank

                @block.tensor
                def _(t, ts=ts, dh=dh, pcol=pcol, k=k):
                    w(t, "s_ld")
                    w(t, "s_g")
                    w(t, "s_dve", dve0v + max(k - 1, 0))
                    for dc in range(8):
                        t.matmul(ps[:, pcol:pcol + 512],
                                 ln2T_rhs(dc, 128 * ts, 128),
                                 wTsb[:, 1024 * dc + 512 * dh:1024 * dc + 512 * (dh + 1)],
                                 start=(dc == 0), stop=False)
                    inc(t.matmul(ps[:, pcol:pcol + 512], ones_row_bf[0:1, 0:128],
                                 bv_row[0:1, 512 * dh:512 * (dh + 1)],
                                 start=False, stop=True), "s_pe")

                @block.vector
                def _(v, dh=dh, pcol=pcol):
                    w(v, "s_pe")
                    if dh == 0:
                        w(v, "s_sc")
                    inc(v.tensor_copy(qstage[:, 512 * dh:512 * (dh + 1)], ps[:, pcol:pcol + 512]), "s_dve")

            @block.scalar
            def _(s, ts=ts):
                w(s, "s_dve")
                inc(s.dma_start(out=v_in[128 * ts:128 * (ts + 1), :], in_=qstage[:, 0:1024]), "s_sc", 16)

        @block.gpsimd
        def _(g):
            w(g, "s_sc")
            inc(g.collective_compute("AllGather", ALU.bypass,
                                     replica_groups=[[0, 1], [2, 3], [4, 5], [6, 7]],
                                     ins=[v_in], outs=[v_out]), "s_cc", 1)

        @block.sync
        def _(sy):
            w(sy, "s_pe")
            inc(sy.dma_start(out=wTsb.rearrange("p (dc o) -> p dc o", o=D),
                             in_=wT_q.rearrange("(dc p) o -> p dc o", p=128)), "s_ld", 16)

        dve0q = cnt["s_dve"]
        for ch in range(8):
            for ts in range(8):
                bank = (ch * 8 + ts) % 2
                pcol = 1024 * bank

                @block.tensor
                def _(t, ch=ch, ts=ts, pcol=pcol):
                    w(t, "s_ld")
                    w(t, "s_dve", dve0q + max(ch * 8 + ts - 1, 0))
                    for dc in range(8):
                        inc(t.matmul(ps[:, pcol:pcol + 512],
                                     wTsb[:, 1024 * dc + 128 * ch:1024 * dc + 128 * (ch + 1)],
                                     ln2T_rhs(dc, 512 * ts, 512),
                                     start=(dc == 0), stop=(dc == 7)), "s_pe")

                @block.vector
                def _(v, ch=ch, ts=ts, pcol=pcol):
                    w(v, "s_pe")
                    if ts == 0:
                        w(v, "s_sc")
                    inc(v.tensor_scalar(qstage[:, 512 * ts:512 * (ts + 1)],
                                        ps[:, pcol:pcol + 512], bq_sb[:, ch:ch + 1], None,
                                        ALU.add), "s_dve")

            @block.scalar
            def _(s, ch=ch):
                w(s, "s_dve")
                inc(s.dma_start(out=qt_dram[128 * ch:128 * (ch + 1), :], in_=qstage), "s_sc", 16)

        barrier()

        escope("a_kvq", _sid)
        _sid = scope("a_core")
        attnT = av(0, 128, 32768)
        attnT_lo = arena[0:64, 0:32768]
        attnT_hi = arena[64:128, 0:32768]
        qt_sb = [av(32768, 128, 4096), av(36864, 128, 4096)]
        kt_sb = [av(40960, 128, 8192), av(49152, 128, 8192)]
        v_sb = [av(57344, 128, 8320), av(65664, 128, 8320)]
        P_sb = av(73984, 128, 2048)
        lb_sb = av(76032, 128, 1024, F32)
        linvAB = av(78080, 1, 2048, F32)

        SA = ps[:, 0:1024]
        SB = ps[:, 1024:2048]
        PVA = ps[0:65, 2048:3072]
        PVB = ps[0:65, 3072:4096]
        LBA = ps[0:64, 0:1024]
        LBB = ps[0:64, 1024:2048]

        @block.vector
        def _(v):
            inc(v.memset(v_sb[0][:], 1.0), "s_dve")
            inc(v.memset(v_sb[1][:], 1.0), "s_dve")
        ones_done = cnt["s_dve"]

        for hp in range(NP):
            pb = hp % 2

            lsem = "s_lda" if pb == 0 else "s_ldb"

            @block.sync
            def _(sy, hp=hp, pb=pb, lsem=lsem):
                w(sy, "s_cc")
                if hp >= 2:
                    w(sy, "s_act")
                if hp < 2:
                    w(sy, "s_dve", ones_done)
                inc(sy.dma_start(out=kt_sb[pb][:, 0:4096], in_=kt_out[128 * hp:128 * (hp + 1), :]), lsem, 16)
                inc(sy.dma_start(out=kt_sb[pb][:, 4096:8192],
                                 in_=kt_out[1024 + 128 * hp:1024 + 128 * (hp + 1), :]), lsem, 16)
                inc(sy.dma_start(out=qt_sb[pb][:], in_=qt_dram[128 * hp:128 * (hp + 1), :]), lsem, 16)
                for gblk in range(8):
                    for h in range(2):
                        inc(sy.dma_start(
                            out=v_sb[pb].rearrange("p (c j) -> p c j", j=130)[
                                :, 8 * gblk:8 * (gblk + 1), 65 * h:65 * h + 64],
                            in_=v_out.rearrange("(c p) d -> p c d", p=128)[
                                :, 8 * gblk:8 * (gblk + 1),
                                128 * hp + 64 * h:128 * hp + 64 * (h + 1)]), lsem, 16)

            ld_pair = cnt[lsem]

            for qt in range(2 * NQT):
                qp = qt % 2
                pvo = 2048 + 1024 * qp
                act0 = cnt["s_act"]
                qk_mark = {}
                for c in range(NKC + 1):
                    @block.tensor
                    def _(t, hp=hp, pb=pb, qt=qt, c=c, ld_pair=ld_pair, lsem=lsem,
                          act0=act0, pvo=pvo):
                        if c == 0:
                            w(t, lsem, ld_pair)
                            w(t, "s_act")
                            w(t, "s_dve")
                        if c >= 1:
                            pp = 1024 * ((c - 1) % 2)
                            t.wait_ge(sems["s_act"], act0 + c)
                            mmpv = t.matmul(ps[0:65, pvo:pvo + 512],
                                            v_sb[pb][:, 130 * (c - 1):130 * (c - 1) + 65],
                                            P_sb[:, pp:pp + 512],
                                            start=(c == 1), stop=(c == NKC))
                            inc(mmpv, "s_pe")
                            mmpv = t.matmul(ps[0:65, pvo + 512:pvo + 1024],
                                            v_sb[pb][:, 130 * (c - 1) + 65:130 * (c - 1) + 130],
                                            P_sb[:, pp + 512:pp + 1024],
                                            start=(c == 1), stop=(c == NKC))
                            inc(mmpv, "s_pe")
                        if c <= NKC - 1:
                            sp = 1024 * (c % 2)
                            for hh in range(2):
                                mmqk = t.matmul(ps[:, sp + 512 * hh:sp + 512 * (hh + 1)],
                                                kt_sb[pb][64 * hh:64 * (hh + 1), 128 * c:128 * (c + 1)],
                                                qt_sb[pb][64 * hh:64 * (hh + 1),
                                                          512 * qt:512 * (qt + 1)],
                                                start=True, stop=True)
                            inc(mmqk, "s_pe")
                            qk_mark[c] = cnt["s_pe"]
                pv_end = cnt["s_pe"]

                @block.scalar
                def _(s, qk_mark=dict(qk_mark)):
                    for c in range(NKC):
                        sp = 1024 * (c % 2)
                        s.wait_ge(sems["s_pe"], qk_mark[c])
                        s.activation(P_sb[:, sp:sp + 1024], ps[:, sp:sp + 1024],
                                     AF.Exp, scale=0.125).then_inc(sems["s_act"], 1)
                cnt["s_act"] += NKC

                @block.vector
                def _(v, pv_end=pv_end, pvo=pvo):
                    v.wait_ge(sems["s_pe"], pv_end)
                    inc(v.reciprocal(linvAB[0:1, 0:1024], ps[64:65, pvo:pvo + 1024]), "s_dve")
                rc_mark = cnt["s_dve"]

                @block.tensor
                def _(t, rc_mark=rc_mark):
                    t.wait_ge(sems["s_dve"], rc_mark)
                    inc(t.matmul(ps[0:64, 0:512], ones64_f[:],
                                 linvAB[0:1, 0:512], start=True, stop=True), "s_pe")
                    inc(t.matmul(ps[0:64, 512:1024], ones64_f[:],
                                 linvAB[0:1, 512:1024], start=True, stop=True), "s_pe")

                @block.vector
                def _(v, hp=hp, qt=qt, pvo=pvo):
                    w(v, "s_pe")
                    col = 4096 * hp + 512 * qt
                    inc(v.tensor_copy(lb_sb[0:64, 0:512], ps[0:64, 0:512]), "s_dve")
                    w(v, "s_dve")
                    inc(v.tensor_tensor(attnT_lo[:, col:col + 512],
                                        ps[0:64, pvo:pvo + 512], lb_sb[0:64, 0:512], ALU.mult), "s_dve")
                    w(v, "s_dve")
                    inc(v.tensor_copy(lb_sb[0:64, 0:512], ps[0:64, 512:1024]), "s_dve")
                    w(v, "s_dve")
                    inc(v.tensor_tensor(attnT_hi[:, col:col + 512],
                                        ps[0:64, pvo + 512:pvo + 1024], lb_sb[0:64, 0:512], ALU.mult), "s_dve")

        barrier()

        escope("a_core", _sid)
        _sid = scope("a_oproj")
        wTo_sb = av(84224, 128, 8192)
        xa3 = [fsv(0, 128, 1024), fsv(1024, 128, 1024)]
        xn3 = [fsv(2048, 128, 1024), fsv(3072, 128, 1024)]

        @block.sync
        def _(sy):
            inc(sy.dma_start(out=wTo_sb.rearrange("p (dc o) -> p dc o", o=D),
                             in_=wT_o.rearrange("(dc p) o -> p dc o", p=128)), "s_ld", 16)

        for i in range(NTILE):
            pb = i % 2
            pcol = 1024 * pb

            lsem = "s_lda" if pb == 0 else "s_ldb"

            @block.sync
            def _(sy, i=i, pb=pb, lsem=lsem):
                if i >= 2:
                    w(sy, "s_dve")
                inc(sy.dma_start(out=xa3[pb], in_=x_src[128 * i:128 * (i + 1), :]), lsem, 16)

            @block.tensor
            def _(t, i=i, pcol=pcol, lsem=lsem):
                w(t, "s_ld")
                w(t, lsem)
                if i >= 2:
                    w(t, "s_dve")
                for dh in range(2):
                    t.matmul(ps[:, pcol + 512 * dh:pcol + 512 * (dh + 1)],
                             ones_row_bf[0:1, 0:128], bo_row[0:1, 512 * dh:512 * (dh + 1)],
                             start=True, stop=False)
                    mm = None
                    for hp in range(NP):
                        mm = t.matmul(ps[:, pcol + 512 * dh:pcol + 512 * (dh + 1)],
                                      attnT[:, 4096 * hp + 128 * i:4096 * hp + 128 * (i + 1)],
                                      wTo_sb[:, 1024 * hp + 512 * dh:1024 * hp + 512 * (dh + 1)],
                                      start=False, stop=(hp == NP - 1))
                    inc(mm, "s_pe")

            @block.vector
            def _(v, i=i, pb=pb, pcol=pcol):
                w(v, "s_pe")
                if i >= 2:
                    w(v, "s_sc")
                inc(v.tensor_tensor(xn3[pb][:, 0:512], ps[:, pcol:pcol + 512], xa3[pb][:, 0:512], ALU.add), "s_dve")
                inc(v.tensor_tensor(xn3[pb][:, 512:1024], ps[:, pcol + 512:pcol + 1024],
                                    xa3[pb][:, 512:1024], ALU.add), "s_dve")

            @block.scalar
            def _(s, i=i, pb=pb):
                w(s, "s_dve")
                inc(s.dma_start(out=x_dst[128 * i:128 * (i + 1), :], in_=xn3[pb]), "s_sc", 16)

        @block.scalar
        def _(s):
            w(s, "s_sc")
        barrier()
        escope("a_oproj", _sid)

    escope("pre", _sid0)
    emit_ffn(0, x_in, x_sp1 if stage >= 2 else out, lg_in[0], lg_out[0])
    if stage >= 2:
        emit_attention(x_sp1, x_sp2 if stage >= 3 else out)
    if stage >= 3:
        emit_ffn(1, x_sp2, out, lg_in[1], lg_out[1])

    @block.scalar
    def _(s):
        w(s, "s_sc")

    es.close()
    return nc


def kernel(**inputs):
    x = np.ascontiguousarray(np.asarray(inputs["L0_attn_hidden_states"], dtype=np.float32))
    B, T, Dm = x.shape
    xf = x.reshape(NCORES, NT, Dm)
    Wqv = np.ascontiguousarray(inputs["L0_attn_Wq"], dtype=np.float32)
    Wkv = np.ascontiguousarray(inputs["L0_attn_Wk"], dtype=np.float32)
    Wvv = np.ascontiguousarray(inputs["L0_attn_Wv"], dtype=np.float32)
    Wov = np.ascontiguousarray(inputs["L0_attn_Wo"], dtype=np.float32)
    bqv = np.ascontiguousarray(np.asarray(inputs["L0_attn_bq"], dtype=np.float32).reshape(8, 128).T)
    bkv = np.ascontiguousarray(np.asarray(inputs["L0_attn_bk"], dtype=np.float32).reshape(8, 128).T)
    bvv = np.asarray(inputs["L0_attn_bv"], dtype=np.float32).reshape(1, D)
    bov = np.asarray(inputs["L0_attn_bo"], dtype=np.float32).reshape(1, D)
    rwv = np.ascontiguousarray(inputs["L0_ffn_router_w"], dtype=np.float32)
    w1v = np.ascontiguousarray(inputs["L0_ffn_w1"], dtype=np.float32)
    b1v = np.asarray(inputs["L0_ffn_b1"], dtype=np.float32).reshape(1, E * FF)
    w2v = np.ascontiguousarray(inputs["L0_ffn_w2"], dtype=np.float32)
    b2v = np.asarray(inputs["L0_ffn_b2"], dtype=np.float32).reshape(1, E * D)

    nc = build_nc(stage=3)
    in_maps = []
    for c in range(NCORES):
        cc = np.zeros((1, 8), np.float32)
        cc[0, 0] = c
        in_maps.append(dict(
            x_in=xf[c], Wq=Wqv, Wk=Wkv, Wv=Wvv, Wo=Wov, bq=bqv, bk=bkv, bv=bvv,
            bo=bov, rw=rwv, w1a=w1v, b1a=b1v, w2a=w2v, b2a=b2v, ccin=cc))
    res = run_bass_kernel_spmd(nc, in_maps, core_ids=list(range(NCORES)))
    outs = [res.results[c]["out"] for c in range(NCORES)]
    return np.concatenate(outs, axis=0).reshape(B, T, Dm)


# version 05
# speedup vs baseline: 1.2439x; 1.2439x over previous
import numpy as np

try:
    import concourse.bass as bass
except ImportError:
    import sys
    sys.path.insert(0, "/opt/trn_rl_repo")
    import concourse.bass as bass
import concourse.mybir as mybir
from concourse.bass import IndirectOffsetOnAxis
from concourse.bass_utils import run_bass_kernel_spmd
import bass_rust
from concourse.engine_type import EngineType

import collections
import re

_ENG_ET = {"SP": EngineType.SP, "Activation": EngineType.Activation,
           "PE": EngineType.PE, "DVE": EngineType.DVE, "Pool": EngineType.Pool}


def _free_dyn_regs(nc, prefix):
    tail = collections.deque(nc.all_instructions(), maxlen=6)
    pat = re.compile(r"R\[(" + prefix + r"_[A-Za-z0-9_]+)\]")
    for ins in tail:
        for nm in set(pat.findall(str(ins))):
            if "_snap_" in nm or nm.startswith(prefix + "_tmp_"):
                try:
                    nc.free_register(bass_rust.RegisterHandle(nm, _ENG_ET[prefix]))
                except Exception:
                    pass


from contextlib import contextmanager


@contextmanager
def _fori(nc, eng, prefix, start, end):
    n0 = sum(1 for _ in nc.all_instructions())
    with eng.Fori(start, end) as i:
        yield i
    pat = re.compile(r"R\[(" + prefix + r"_[A-Za-z0-9_]*_fori_[A-Za-z0-9_]+)\]")
    names = set()
    for k, ins in enumerate(nc.all_instructions()):
        if k < n0:
            continue
        names |= set(pat.findall(str(ins)))
    for nm in names:
        try:
            nc.free_register(bass_rust.RegisterHandle(nm, _ENG_ET[prefix]))
        except Exception:
            pass

F32 = mybir.dt.float32
BF16 = mybir.dt.bfloat16
I32 = mybir.dt.int32
U32 = mybir.dt.uint32
AF = mybir.ActivationFunctionType
ALU = mybir.AluOpType
AXX = mybir.AxisListType.X

NCORES = 8
NT = 4096
NTILE = 32
NTOK = 32768
NBLK = 256
D = 1024
FF = 4096
E = 8
CAP = 6144
SECT = 4096
ETILE = 256
S = 8192
NP = 8
NQT = 4
NKC = 64
BIG = 3000000.0
ARN = 98304

PE_T = 66
LDF_T = 16
SCF_T = 16


def build_nc(stage=3):
    nc = bass.Bass("TRN2", target_bir_lowering=False)

    x_in = nc.dram_tensor("x_in", [NT, D], F32, kind="ExternalInput").ap()
    Wq = nc.dram_tensor("Wq", [D, D], F32, kind="ExternalInput").ap()
    Wk = nc.dram_tensor("Wk", [D, D], F32, kind="ExternalInput").ap()
    Wv = nc.dram_tensor("Wv", [D, D], F32, kind="ExternalInput").ap()
    Wo = nc.dram_tensor("Wo", [D, D], F32, kind="ExternalInput").ap()
    bq = nc.dram_tensor("bq", [128, 8], F32, kind="ExternalInput").ap()
    bk = nc.dram_tensor("bk", [128, 8], F32, kind="ExternalInput").ap()
    bv = nc.dram_tensor("bv", [1, D], F32, kind="ExternalInput").ap()
    bo = nc.dram_tensor("bo", [1, D], F32, kind="ExternalInput").ap()
    rw = nc.dram_tensor("rw", [E, D], F32, kind="ExternalInput").ap()
    w1a = nc.dram_tensor("w1a", [E, D, FF], F32, kind="ExternalInput").ap()
    b1a = nc.dram_tensor("b1a", [1, E * FF], F32, kind="ExternalInput").ap()
    w2a = nc.dram_tensor("w2a", [E, FF, D], F32, kind="ExternalInput").ap()
    b2a = nc.dram_tensor("b2a", [1, E * D], F32, kind="ExternalInput").ap()
    ccin = nc.dram_tensor("ccin", [1, 8], F32, kind="ExternalInput").ap()
    out = nc.dram_tensor("out", [NT, D], F32, kind="ExternalOutput").ap()

    wT_q = nc.dram_tensor("wT_q", [D, D], BF16).ap()
    wT_k = nc.dram_tensor("wT_k", [D, D], BF16).ap()
    wT_v = nc.dram_tensor("wT_v", [D, D], BF16).ap()
    wT_o = nc.dram_tensor("wT_o", [D, D], BF16).ap()
    ln_dram = nc.dram_tensor("ln_dram", [NT, D], BF16).ap()
    xg_dram = nc.dram_tensor("xg_dram", [E * SECT + ETILE, D], BF16).ap()
    eo_dram = nc.dram_tensor("eo_dram", [E * SECT + ETILE, D], BF16).ap()
    x_sp1 = nc.dram_tensor("x_sp1", [NT, D], F32).ap()
    x_sp2 = nc.dram_tensor("x_sp2", [NT, D], F32).ap()
    qt_dram = nc.dram_tensor("qt_dram", [NP * 128, NT], BF16).ap()
    pos_dram = nc.dram_tensor("pos_dram", [NBLK, 128], F32).ap()
    e_dram = nc.dram_tensor("e_dram", [NBLK, 128], F32).ap()
    p_dram = nc.dram_tensor("p_dram", [128, NBLK], F32).ap()
    B_dram = nc.dram_tensor("B_dram", [NBLK + 1, E], F32).ap()
    lg_in = [nc.dram_tensor(f"lg_in{a}", [NT, E], F32).ap() for a in (0, 1)]
    lg_out = [nc.dram_tensor(f"lg_out{a}", [NTOK, E], F32).ap() for a in (0, 1)]
    kt_in = nc.dram_tensor("kt_in", [D, NT], BF16).ap()
    kt_out = nc.dram_tensor("kt_out", [2 * D, NT], BF16).ap()
    v_in = nc.dram_tensor("v_in", [NT, D], BF16).ap()
    v_out = nc.dram_tensor("v_out", [S, D], BF16).ap()

    from contextlib import ExitStack
    es = ExitStack()
    ec = es.enter_context

    arena = ec(nc.sbuf_tensor("arena", [128, ARN], BF16))
    ident_bf = ec(nc.sbuf_tensor("ident_bf", [128, 128], BF16))
    ident_f = ec(nc.sbuf_tensor("ident_f", [128, 128], F32))
    lt_bf = ec(nc.sbuf_tensor("lt_bf", [128, 128], BF16))
    iota_col = ec(nc.sbuf_tensor("iota_col", [128, 1], F32))
    iota_row = ec(nc.sbuf_tensor("iota_row", [128, 128], F32))
    ones_col_bf = ec(nc.sbuf_tensor("ones_col_bf", [128, 1], BF16))
    ones_row_bf = ec(nc.sbuf_tensor("ones_row_bf", [1, 512], BF16))
    ones64_f = ec(nc.sbuf_tensor("ones64_f", [1, 64], F32))
    onesf_row = ec(nc.sbuf_tensor("onesf_row", [1, 128], F32))
    zeros128 = ec(nc.sbuf_tensor("zeros128", [128, 128], F32))
    cc_f = ec(nc.sbuf_tensor("cc_f", [1, 8], F32))
    cc_i = ec(nc.sbuf_tensor("cc_i", [1, 8], I32))
    bq_sb = ec(nc.sbuf_tensor("bq_sb", [128, 8], F32))
    bk_sb = ec(nc.sbuf_tensor("bk_sb", [128, 8], F32))
    rwT = ec(nc.sbuf_tensor("rwT", [128, E * 8], BF16))
    cnts_f = ec(nc.sbuf_tensor("cnts_f", [1, E], F32))
    cnts_i = ec(nc.sbuf_tensor("cnts_i", [1, E], I32))
    scat_u = ec(nc.sbuf_tensor("scat_u", [128, NTILE], U32))
    gath_u = ec(nc.sbuf_tensor("gath_u", [128, NTILE], U32))
    gsc_sb = ec(nc.sbuf_tensor("gsc_sb", [128, NTILE], F32))

    ps = ec(nc.psum_tensor("ps", [128, 4096], F32))

    semnames = ("s_bar", "s_rel", "s_ld", "s_lda", "s_ldb", "s_ldf", "s_sc", "s_scf", "s_g", "s_g2",
                "s_w1", "s_w2", "s_cc", "s_pe",
                "s_act", "s_dve", "s_gp", "s_pef", "s_actf", "s_dvef", "s_dvef2")
    sems = {n: ec(nc.semaphore(n)) for n in semnames}
    block = ec(nc.Block())
    cnt = {n: 0 for n in semnames}

    def inc(instr, name, n=1):
        cnt[name] += n
        return instr.then_inc(sems[name], n)

    def w(eng, name, val=None):
        eng.wait_ge(sems[name], cnt[name] if val is None else val)

    def av(off, parts, cols, dtype=BF16):
        width = cols * (2 if dtype in (F32, I32, U32) else 1)
        vw = arena[0:parts, off:off + width]
        if dtype != BF16:
            vw = vw.bitcast(dtype)
        return vw

    regs = {}
    def reg(eng, name):
        key = (eng.engine, name)
        if key not in regs:
            regs[key] = eng.alloc_register(name + '_r')
        return regs[key]

    barno = [0]

    def barrier():
        barno[0] += 1
        k = barno[0]

        @block.tensor
        def _(t_):
            t_.nop().then_inc(sems["s_bar"], 1)

        @block.scalar
        def _(s_):
            w(s_, "s_sc")
            s_.nop().then_inc(sems["s_bar"], 1)

        @block.vector
        def _(v_):
            v_.nop().then_inc(sems["s_bar"], 1)

        @block.gpsimd
        def _(g_):
            w(g_, "s_g"); w(g_, "s_g2"); w(g_, "s_w1"); w(g_, "s_w2")
            g_.nop().then_inc(sems["s_bar"], 1)

        @block.sync
        def _(sy):
            w(sy, "s_ld"); w(sy, "s_lda"); w(sy, "s_ldb")
            sy.wait_ge(sems["s_bar"], 4 * k)
            sy.nop().then_inc(sems["s_rel"], 1)

        for eng in ("tensor", "scalar", "vector", "gpsimd"):
            @getattr(block, eng)
            def _(e_):
                e_.wait_ge(sems["s_rel"], k)

    trps = ps[:, 3072:3584].bitcast(BF16)
    ASCR0 = 40960
    FS = 53248
    def fsv(k, parts, n, dtype=F32):
        return av(FS + 2 * k, parts, n, dtype)
    lnrot = [av(45056, 128, 1024), av(46080, 128, 1024)]
    ztile = av(ASCR0 + 7168, 128, 2048)
    iota8 = fsv(16000, 128, 2048)
    iota8m8 = fsv(18048, 128, 2048)

    def scope(name):
        return nc.enter_named_scope(name, False)[0]

    def escope(name, sid):
        nc.leave_named_scope(name, sid, False)

    _sid0 = scope("pre")
    @block.sync
    def _(sy):
        inc(sy.dma_start(out=cc_f[:], in_=ccin), "s_ld", 16)
        inc(sy.dma_start(out=bq_sb[:], in_=bq), "s_ld", 16)
        inc(sy.dma_start(out=bk_sb[:], in_=bk), "s_ld", 16)

    @block.gpsimd
    def _(g):
        g.reg_mov(reg(g, "bchk"), E * SECT + ETILE - 1)
        inc(g.iota(iota_col[:], pattern=[[0, 1]], base=0, channel_multiplier=1,
                   allow_small_or_imprecise_dtypes=True), "s_gp")
        inc(g.iota(iota_row[:], pattern=[[1, 128]], base=0, channel_multiplier=0,
                   allow_small_or_imprecise_dtypes=True), "s_gp")

    @block.vector
    def _(v):
        w(v, "s_gp")
        w(v, "s_ld")
        inc(v.tensor_scalar(ident_bf[:], iota_row[:], iota_col[:], None, ALU.is_equal), "s_dve")
        inc(v.tensor_scalar(ident_f[:], iota_row[:], iota_col[:], None, ALU.is_equal), "s_dve")
        inc(v.tensor_scalar(lt_bf[:], iota_row[:], iota_col[:], None, ALU.is_gt), "s_dve")
        inc(v.memset(ones_col_bf[:], 1.0), "s_dve")
        inc(v.memset(ones_row_bf[:], 1.0), "s_dve")
        inc(v.memset(ones64_f[:], 1.0), "s_dve")
        inc(v.memset(onesf_row[:], 1.0), "s_dve")
        inc(v.memset(zeros128[:], 0.0), "s_dve")
        inc(v.memset(ztile[:], 0.0), "s_dve")
        inc(v.tensor_copy(cc_i[:], cc_f[:]), "s_dve")

    @block.scalar
    def _(s):
        w(s, "s_dve")
        nrows = E * SECT + ETILE
        for dst in (xg_dram, eo_dram):
            for i in range(0, nrows, 256):
                n = min(256, nrows - i)
                inc(s.dma_start(out=dst[i:i + n, :].rearrange("(a p) d -> p a d", p=128),
                                in_=ztile[:].rearrange("p (a d) -> p a d", d=D)[:, 0:n // 128, :]),
                    "s_sc", 16)

    ASCR = ASCR0
    w32 = av(ASCR, 128, 1024, F32)
    wbf0 = av(ASCR + 2048, 128, 1024)
    wtT0 = av(ASCR + 3072, 128, 1024)

    for (Wsrc, Wdst) in ((Wq, wT_q), (Wk, wT_k), (Wv, wT_v), (Wo, wT_o)):
        for r in range(8):
            @block.sync
            def _(sy, Wsrc=Wsrc, r=r):
                w(sy, "s_dve")
                inc(sy.dma_start(out=w32, in_=Wsrc[128 * r:128 * (r + 1), :]), "s_ld", 16)

            @block.vector
            def _(v):
                w(v, "s_ld")
                w(v, "s_pe")
                inc(v.tensor_copy(wbf0, w32), "s_dve")

            @block.tensor
            def _(t):
                w(t, "s_dve")
                for cq in range(8):
                    inc(t.transpose(trps[:, 128 * cq:128 * (cq + 1)],
                                    wbf0[:, 128 * cq:128 * (cq + 1)], ident_bf[:]), "s_pe")

            @block.vector
            def _(v):
                w(v, "s_pe")
                w(v, "s_sc")
                inc(v.tensor_copy(wtT0, trps), "s_dve")

            @block.scalar
            def _(s, Wdst=Wdst, r=r):
                w(s, "s_dve")
                inc(s.dma_start(
                    out=Wdst.rearrange("(cq p) o -> p cq o", p=128)[:, :, 128 * r:128 * (r + 1)],
                    in_=wtT0.rearrange("p (cq o) -> p cq o", o=128)), "s_sc", 16)

    rw32 = av(ASCR + 4096, 8, 1024, F32)
    rwbf = av(ASCR + 6144, 8, 1024)

    @block.sync
    def _(sy):
        w(sy, "s_dve")
        inc(sy.dma_start(out=rw32, in_=rw), "s_ld", 16)

    @block.vector
    def _(v):
        w(v, "s_ld")
        w(v, "s_pe")
        inc(v.tensor_copy(rwbf, rw32), "s_dve")

    for ch in range(8):
        @block.tensor
        def _(t, ch=ch):
            w(t, "s_dve")
            inc(t.transpose(ps[:, 3072:3076].bitcast(BF16), rwbf[:, 128 * ch:128 * (ch + 1)],
                            ident_bf[0:8, 0:8]), "s_pe")

        @block.vector
        def _(v, ch=ch):
            w(v, "s_pe")
            inc(v.tensor_copy(rwT[:, 8 * ch:8 * (ch + 1)],
                              ps[:, 3072:3076].bitcast(BF16)[:, 0:8]), "s_dve")

    @block.scalar
    def _(s):
        w(s, "s_sc")
    @block.gpsimd
    def _(g):
        w(g, "s_g")
    barrier()

    @block.sync
    def _(sy):
        sy.reg_mov(reg(sy, "m_pef"), 0)
    @block.tensor
    def _(t):
        t.reg_mov(reg(t, "m_ldf"), 0)
        t.reg_mov(reg(t, "m_actf"), 0)
        t.reg_mov(reg(t, "m_dvef"), 0)
        t.reg_mov(reg(t, "m_dvef2"), 0)
    @block.scalar
    def _(s):
        s.reg_mov(reg(s, "m_pef"), 0)
        s.reg_mov(reg(s, "m_dvef2"), 0)
        s.reg_mov(reg(s, "m_scf"), 0)
    @block.vector
    def _(v):
        v.reg_mov(reg(v, "m_pef"), 0)
        v.reg_mov(reg(v, "m_scf"), 0)
    @block.gpsimd
    def _(g):
        g.reg_mov(reg(g, "m_pef"), 0)
        g.reg_mov(reg(g, "m_pef2"), 0)
        g.reg_mov(reg(g, "m_scf"), 0)

    def emit_ln(x_src, with_logits, ln2T=None):
        xa = [fsv(0, 128, 1024), fsv(1024, 128, 1024)]
        xsq = fsv(2048, 128, 1024)
        stat = fsv(3072, 128, 16)
        xhat = fsv(3088, 128, 1024)
        hiT = av(FS + 2 * 4112, 128, 1024)
        lgsb = fsv(4624, 128, 256)
        lgps = ps[:, 3584:3592]

        for i in range(NTILE):
            pb = i % 2

            lsem = "s_lda" if pb == 0 else "s_ldb"

            @block.sync
            def _(sy, i=i, pb=pb, lsem=lsem):
                if i >= 2:
                    w(sy, "s_act")
                inc(sy.dma_start(out=xa[pb], in_=x_src[128 * i:128 * (i + 1), :]), lsem, 16)

            @block.scalar
            def _(s, pb=pb, lsem=lsem):
                w(s, lsem)
                w(s, "s_dve")
                inc(s.activation(xsq, xa[pb], AF.Square), "s_act")

            @block.vector
            def _(v, pb=pb, lsem=lsem):
                w(v, lsem)
                inc(v.tensor_reduce(stat[:, 0:1], xa[pb], AXX, ALU.add, negate=True), "s_dve")
                w(v, "s_act")
                inc(v.tensor_reduce(stat[:, 1:2], xsq, AXX, ALU.add), "s_dve")
                w(v, "s_dve")
                inc(v.tensor_scalar(stat[:, 2:3], stat[:, 0:1], 1.0 / D, None, ALU.mult), "s_dve")
                w(v, "s_dve")
                inc(v.tensor_scalar(stat[:, 3:4], stat[:, 1:2], 1.0 / D, None, ALU.mult), "s_dve")
                w(v, "s_dve")
                inc(v.tensor_tensor(stat[:, 4:5], stat[:, 2:3], stat[:, 2:3], ALU.mult), "s_dve")
                w(v, "s_dve")
                inc(v.tensor_scalar(stat[:, 5:6], stat[:, 3:4], stat[:, 4:5], 1e-5,
                                    ALU.subtract, ALU.add), "s_dve")

            @block.scalar
            def _(s):
                w(s, "s_dve")
                inc(s.activation(stat[:, 6:7], stat[:, 5:6], AF.Sqrt), "s_act")

            @block.vector
            def _(v):
                w(v, "s_act")
                inc(v.reciprocal(stat[:, 7:8], stat[:, 6:7]), "s_dve")
                w(v, "s_dve")
                inc(v.tensor_tensor(stat[:, 8:9], stat[:, 2:3], stat[:, 7:8], ALU.mult), "s_dve")

            @block.scalar
            def _(s, pb=pb):
                w(s, "s_dve")
                inc(s.activation(xhat, xa[pb], AF.Identity, bias=stat[:, 8:9], scale=stat[:, 7:8]), "s_act")

            @block.vector
            def _(v, pb=pb):
                w(v, "s_act")
                if with_logits:
                    w(v, "s_sc")
                else:
                    w(v, "s_pe")
                inc(v.tensor_copy(lnrot[pb], xhat), "s_dve")

            if with_logits:
                @block.scalar
                def _(s, i=i, pb=pb):
                    w(s, "s_dve")
                    inc(s.dma_start(out=ln_dram[128 * i:128 * (i + 1), :], in_=lnrot[pb]), "s_sc", 16)

            @block.tensor
            def _(t, pb=pb):
                w(t, "s_dve")
                for cq in range(8):
                    inc(t.transpose(trps[:, 128 * cq:128 * (cq + 1)],
                                    lnrot[pb][:, 128 * cq:128 * (cq + 1)], ident_bf[:]), "s_pe")

            if with_logits:
                @block.vector
                def _(v):
                    w(v, "s_pe")
                    inc(v.tensor_copy(hiT, trps), "s_dve")

                @block.tensor
                def _(t):
                    w(t, "s_dve")
                    for dc in range(8):
                        inc(t.matmul(lgps, hiT[:, 128 * dc:128 * (dc + 1)],
                                     rwT[:, 8 * dc:8 * (dc + 1)],
                                     start=(dc == 0), stop=(dc == 7)), "s_pe")

                @block.vector
                def _(v, i=i):
                    w(v, "s_pe")
                    inc(v.tensor_copy(lgsb[:, 8 * i:8 * (i + 1)], lgps), "s_dve")
            else:
                @block.vector
                def _(v, i=i):
                    w(v, "s_pe")
                    inc(v.tensor_copy(
                        ln2T[:, 1024 * i:1024 * (i + 1)].rearrange("p (cq o) -> p cq o", o=128),
                        trps.rearrange("p (cq o) -> p cq o", o=128)), "s_dve")
        return lgsb

    def emit_ffn(app, x_src, y_dst, lgi, lgo):
        _sid = scope(f"f{app}_lnlog")
        lgsb = emit_ln(x_src, with_logits=True)

        @block.scalar
        def _(s):
            w(s, "s_dve")
            inc(s.dma_start(out=lgi.rearrange("(i p) e -> p i e", p=128),
                            in_=lgsb.rearrange("p (i e) -> p i e", e=8)), "s_sc", 16)

        @block.gpsimd
        def _(g):
            w(g, "s_sc")
            inc(g.collective_compute("AllGather", ALU.bypass,
                                     replica_groups=[list(range(NCORES))],
                                     ins=[lgi], outs=[lgo]), "s_cc", 1)

        barrier()

        escope(f"f{app}_lnlog", _sid)
        _sid = scope(f"f{app}_route")
        lgT = fsv(0, 128, 2048)
        mx = fsv(2048, 128, 256)
        eqh = fsv(2304, 128, 2048)
        t2 = fsv(4352, 128, 2048)
        e_tm = fsv(6400, 128, 256)
        ex = fsv(6656, 128, 2048)
        sume = fsv(8704, 128, 256)
        p_tm = fsv(8960, 128, 256)
        eT = [fsv(9216, 128, 128), fsv(9344, 128, 128)]
        oh = [fsv(9472, 128, 1024), fsv(10496, 128, 1024)]
        incl = [fsv(11520, 128, 1024), fsv(12544, 128, 1024)]
        Tj = fsv(13568, 128, 16)
        Tjbf = av(FS + 2 * 13584, 128, 16)
        Bj = fsv(13600, 128, 16)
        tot = fsv(13632, 1, 24)
        totb = fsv(13664, 128, 8)
        posf = fsv(13696, 128, 1024)
        pos_j = fsv(14720, 128, 256)
        scr = fsv(14976, 128, 1024)

        @block.sync
        def _(sy):
            w(sy, "s_cc")
            inc(sy.dma_start(out=lgT.rearrange("p (i e) -> p i e", e=8),
                             in_=lgo.rearrange("(i p) e -> p i e", p=128)), "s_ld", 16)

        @block.gpsimd
        def _(g):
            inc(g.iota(iota8[:], pattern=[[0, 256], [1, 8]], base=0, channel_multiplier=0,
                       allow_small_or_imprecise_dtypes=True), "s_gp")

        @block.vector
        def _(v):
            w(v, "s_gp")
            w(v, "s_sc")
            inc(v.tensor_scalar(iota8m8[:], iota8[:], -8.0, None, ALU.add), "s_dve")
            w(v, "s_ld")
            inc(v.tensor_reduce(mx, lgT.rearrange("p (i e) -> p i e", e=8), AXX, ALU.max), "s_dve")
            w(v, "s_dve")
            inc(v.tensor_tensor(eqh.rearrange("p (i e) -> p i e", e=8),
                                lgT.rearrange("p (i e) -> p i e", e=8),
                                mx.unsqueeze(-1).broadcast_to((128, 256, 8)),
                                ALU.is_equal), "s_dve")
            w(v, "s_dve")
            inc(v.tensor_tensor(t2, eqh, iota8m8[:], ALU.mult), "s_dve")
            w(v, "s_dve")
            inc(v.tensor_scalar(t2, t2, 8.0, None, ALU.add), "s_dve")
            w(v, "s_dve")
            inc(v.tensor_reduce(e_tm, t2.rearrange("p (i e) -> p i e", e=8), AXX, ALU.min), "s_dve")
            w(v, "s_dve")
            inc(v.tensor_tensor(eqh.rearrange("p (i e) -> p i e", e=8),
                                lgT.rearrange("p (i e) -> p i e", e=8),
                                mx.unsqueeze(-1).broadcast_to((128, 256, 8)),
                                ALU.subtract), "s_dve")

        @block.scalar
        def _(s):
            w(s, "s_dve")
            inc(s.activation(ex, eqh, AF.Exp), "s_act")

        @block.vector
        def _(v):
            w(v, "s_act")
            inc(v.tensor_reduce(sume, ex.rearrange("p (i e) -> p i e", e=8), AXX, ALU.add), "s_dve")
            w(v, "s_dve")
            inc(v.reciprocal(p_tm, sume), "s_dve")

        for j in range(2):
            @block.tensor
            def _(t, j=j):
                w(t, "s_dve")
                inc(t.transpose(ps[:, 0:128], e_tm[:, 128 * j:128 * (j + 1)], ident_f[:]), "s_pe")

            @block.vector
            def _(v, j=j):
                w(v, "s_pe")
                inc(v.tensor_copy(eT[j], ps[:, 0:128]), "s_dve")

        for j in range(2):
            @block.vector
            def _(v, j=j):
                w(v, "s_dve")
                inc(v.tensor_tensor(oh[j].rearrange("p (t e) -> p t e", e=8),
                                    iota8[:, 0:1024].rearrange("p (t e) -> p t e", e=8),
                                    eT[j].unsqueeze(-1).broadcast_to((128, 128, 8)),
                                    ALU.is_equal), "s_dve")
                for e in range(E):
                    w(v, "s_dve")
                    inc(v.tensor_tensor_scan(
                        incl[j].rearrange("p (t e) -> p e t", e=8)[:, e, :],
                        oh[j].rearrange("p (t e) -> p e t", e=8)[:, e, :],
                        zeros128[:], 0.0, ALU.add, ALU.add), "s_dve")
                w(v, "s_dve")
                inc(v.tensor_copy(Tj[:, 8 * j:8 * (j + 1)], incl[j][:, 1016:1024]), "s_dve")
                w(v, "s_dve")
                inc(v.tensor_copy(Tjbf[:, 8 * j:8 * (j + 1)], Tj[:, 8 * j:8 * (j + 1)]), "s_dve")

        @block.tensor
        def _(t):
            w(t, "s_dve")
            inc(t.matmul(ps[:, 0:8], lt_bf[:], Tjbf[:, 0:8], start=True, stop=True), "s_pe")
            inc(t.matmul(ps[:, 512:520], lt_bf[:], Tjbf[:, 8:16], start=True, stop=True), "s_pe")
            inc(t.matmul(ps[0:1, 1024:1032], ones_col_bf[:], Tjbf[:, 0:8], start=True, stop=True), "s_pe")
            inc(t.matmul(ps[0:1, 1536:1544], ones_col_bf[:], Tjbf[:, 8:16], start=True, stop=True), "s_pe")

        @block.vector
        def _(v):
            w(v, "s_pe")
            inc(v.tensor_copy(Bj[:, 0:8], ps[:, 0:8]), "s_dve")
            inc(v.tensor_copy(Bj[:, 8:16], ps[:, 512:520]), "s_dve")
            inc(v.tensor_copy(tot[0:1, 0:8], ps[0:1, 1024:1032]), "s_dve")
            inc(v.tensor_copy(tot[0:1, 8:16], ps[0:1, 1536:1544]), "s_dve")
            w(v, "s_dve")
            inc(v.tensor_tensor(tot[0:1, 16:24], tot[0:1, 0:8], tot[0:1, 8:16], ALU.add), "s_dve")

        @block.tensor
        def _(t):
            w(t, "s_dve")
            inc(t.matmul(ps[:, 2048:2056], onesf_row[:], tot[0:1, 0:8], start=True, stop=True), "s_pe")

        @block.vector
        def _(v):
            w(v, "s_pe")
            inc(v.tensor_copy(totb[:], ps[:, 2048:2056]), "s_dve")
            w(v, "s_dve")
            inc(v.tensor_tensor(Bj[:, 8:16], Bj[:, 8:16], totb[:], ALU.add), "s_dve")

        for j in range(2):
            @block.vector
            def _(v, j=j):
                w(v, "s_dve")
                w(v, "s_sc")
                inc(v.tensor_tensor(posf, incl[j], oh[j], ALU.subtract), "s_dve")
                w(v, "s_dve")
                inc(v.tensor_tensor(posf.rearrange("p (t e) -> p t e", e=8),
                                    posf.rearrange("p (t e) -> p t e", e=8),
                                    Bj[:, 8 * j:8 * (j + 1)].unsqueeze(1).broadcast_to((128, 128, 8)),
                                    ALU.add), "s_dve")
                w(v, "s_dve")
                inc(v.tensor_tensor(scr, posf, oh[j], ALU.mult), "s_dve")
                w(v, "s_dve")
                inc(v.tensor_reduce(pos_j[:, 128 * j:128 * (j + 1)],
                                    scr.rearrange("p (t e) -> p t e", e=8), AXX, ALU.add), "s_dve")

            @block.scalar
            def _(s, j=j):
                w(s, "s_dve")
                inc(s.dma_start(out=pos_dram[128 * j:128 * (j + 1), :],
                                in_=pos_j[:, 128 * j:128 * (j + 1)]), "s_sc", 16)
                inc(s.dma_start(out=e_dram[128 * j:128 * (j + 1), :], in_=eT[j]), "s_sc", 16)

        @block.scalar
        def _(s):
            w(s, "s_dve")
            inc(s.dma_start(out=p_dram, in_=p_tm), "s_sc", 16)
            inc(s.dma_start(out=B_dram[0:128, :], in_=Bj[:, 0:8]), "s_sc", 16)
            inc(s.dma_start(out=B_dram[128:256, :], in_=Bj[:, 8:16]), "s_sc", 16)
            inc(s.dma_start(out=B_dram[256:257, :], in_=tot[0:1, 16:24]), "s_sc", 16)

        barrier()

        posbt = fsv(0, 32, 128)
        ebt = fsv(128, 32, 128)
        e4bt = fsv(256, 32, 128)
        pown = fsv(384, 128, 32)
        B32 = fsv(416, 1, 16)
        B32b = fsv(432, 32, 8)
        ohown = fsv(440, 32, 1024)
        bsel = fsv(1464, 32, 128)
        scatbt = fsv(1592, 32, 128)
        gathbt = fsv(1720, 32, 128)
        posG = fsv(1848, 128, 32)
        scat_f = fsv(1880, 128, 32)
        gath_f = fsv(1912, 128, 32)
        kept = fsv(1944, 128, 32)
        oscr = fsv(1976, 32, 1024)

        @block.sync
        def _(sy):
            r1 = reg(sy, "o1")
            r2 = reg(sy, "o2")
            sy.reg_load(r1, cc_i[0:1, 0:1])
            sy.reg_mul(r2, r1, 32 * 128)
            inc(sy.dma_start(out=posbt, in_=bass.AP(pos_dram.tensor, r2, [[128, 32], [1, 128]])), "s_ld", 16)
            _free_dyn_regs(nc, "SP")
            inc(sy.dma_start(out=ebt, in_=bass.AP(e_dram.tensor, r2, [[128, 32], [1, 128]])), "s_ld", 16)
            _free_dyn_regs(nc, "SP")
            sy.reg_mul(r2, r1, 32)
            inc(sy.dma_start(out=pown, in_=bass.AP(p_dram.tensor, r2, [[256, 128], [1, 32]])), "s_ld", 16)
            _free_dyn_regs(nc, "SP")
            sy.reg_mul(r2, r1, 32 * 8)
            inc(sy.dma_start(out=B32[0:1, 0:8], in_=bass.AP(B_dram.tensor, r2, [[8, 1], [1, 8]])), "s_ld", 16)
            _free_dyn_regs(nc, "SP")
            sy.reg_add(r2, r2, 32 * 8)
            inc(sy.dma_start(out=B32[0:1, 8:16], in_=bass.AP(B_dram.tensor, r2, [[8, 1], [1, 8]])), "s_ld", 16)
            _free_dyn_regs(nc, "SP")

        @block.tensor
        def _(t):
            w(t, "s_ld")
            inc(t.matmul(ps[0:32, 2048:2056], onesf_row[0:1, 0:32], B32[0:1, 0:8], start=True, stop=True), "s_pe")

        @block.vector
        def _(v):
            w(v, "s_pe")
            inc(v.tensor_copy(B32b[:], ps[0:32, 2048:2056]), "s_dve")
            w(v, "s_dve")
            inc(v.tensor_tensor(ohown.rearrange("p (t e) -> p t e", e=8),
                                iota8[0:32, 0:1024].rearrange("p (t e) -> p t e", e=8),
                                ebt.unsqueeze(-1).broadcast_to((32, 128, 8)),
                                ALU.is_equal), "s_dve")
            w(v, "s_dve")
            inc(v.tensor_tensor(oscr.rearrange("p (t e) -> p t e", e=8),
                                ohown.rearrange("p (t e) -> p t e", e=8),
                                B32b[0:32, :].unsqueeze(1).broadcast_to((32, 128, 8)),
                                ALU.mult), "s_dve")
            w(v, "s_dve")
            inc(v.tensor_reduce(bsel, oscr.rearrange("p (t e) -> p t e", e=8), AXX, ALU.add), "s_dve")
            w(v, "s_dve")
            inc(v.tensor_tensor(scatbt, posbt, bsel, ALU.subtract), "s_dve")
            inc(v.tensor_scalar(e4bt, ebt, float(SECT), None, ALU.mult), "s_dve")
            w(v, "s_dve")
            inc(v.tensor_tensor(scatbt, scatbt, e4bt, ALU.add), "s_dve")
            inc(v.tensor_scalar(gathbt, posbt, float(CAP), None, ALU.is_ge), "s_dve")
            w(v, "s_dve")
            inc(v.tensor_scalar(gathbt, gathbt, BIG, None, ALU.mult), "s_dve")
            w(v, "s_dve")
            inc(v.tensor_tensor(gathbt, gathbt, scatbt, ALU.add), "s_dve")
            inc(v.tensor_tensor(cnts_f[:], B32[0:1, 8:16], B32[0:1, 0:8], ALU.subtract), "s_dve")
            w(v, "s_dve")
            inc(v.tensor_copy(cnts_i[:], cnts_f[:]), "s_dve")

        @block.tensor
        def _(t):
            w(t, "s_dve")
            inc(t.transpose(ps[:, 0:32], scatbt, ident_f[0:32, 0:32]), "s_pe")
            inc(t.transpose(ps[:, 512:544], gathbt, ident_f[0:32, 0:32]), "s_pe")
            inc(t.transpose(ps[:, 1024:1056], posbt, ident_f[0:32, 0:32]), "s_pe")

        @block.vector
        def _(v):
            w(v, "s_pe")
            inc(v.tensor_copy(scat_f, ps[:, 0:32]), "s_dve")
            inc(v.tensor_copy(gath_f, ps[:, 512:544]), "s_dve")
            inc(v.tensor_copy(posG, ps[:, 1024:1056]), "s_dve")
            w(v, "s_dve")
            inc(v.tensor_copy(scat_u[:], scat_f), "s_dve")
            inc(v.tensor_copy(gath_u[:], gath_f), "s_dve")
            inc(v.tensor_scalar(kept, posG, float(CAP), None, ALU.is_lt), "s_dve")
            w(v, "s_dve")
            inc(v.tensor_tensor(gsc_sb[:], kept, pown, ALU.mult), "s_dve")
            w(v, "s_dve")
            inc(v.tensor_scalar(gsc_sb[:], gsc_sb[:], 0.5, None, ALU.mult), "s_dve")

        barrier()

        escope(f"f{app}_route", _sid)
        _sid = scope(f"f{app}_scat")
        for i in range(NTILE):
            pb = i % 2

            lsem = "s_lda" if pb == 0 else "s_ldb"

            @block.sync
            def _(sy, i=i, pb=pb, lsem=lsem):
                if i >= 2:
                    w(sy, "s_g")
                inc(sy.dma_start(out=lnrot[pb], in_=ln_dram[128 * i:128 * (i + 1), :]), lsem, 16)

            @block.gpsimd
            def _(g, i=i, pb=pb, lsem=lsem):
                w(g, lsem)
                inc(g.indirect_dma_start(
                    out=xg_dram, out_offset=IndirectOffsetOnAxis(scat_u[:, i:i + 1], 0),
                    in_=lnrot[pb], in_offset=None), "s_g", 16)

        barrier()

        escope(f"f{app}_scat", _sid)
        _sid = scope(f"f{app}_exp")
        w1sb = av(0, 128, 32768)
        w2sb = av(32768, 128, 32768)
        xgT = av(65536, 128, 2048)
        xgtok = av(67584, 128, 2048)
        hb = [av(69632, 128, 256), av(69888, 128, 256)]
        eosb = av(70144, 128, 2048)
        b1cur = [av(73216, 1, FF), av(78336, 1, FF)]
        b2cur = [av(77312, 1, D), av(82432, 1, D)]
        hps = [ps[:, 0:256], ps[:, 512:768]]
        ops_ = [ps[:, 1024:2048], ps[:, 2048:3072]]
        trp2 = ps[:, 3072:3584].bitcast(BF16)

        for e in range(E):
            w1_base = cnt["s_w1"]
            w2_base = cnt["s_w2"]

            @block.gpsimd
            def _(g, e=e, app=app):
                mp = reg(g, "m_pef")
                mp2 = reg(g, "m_pef2")
                t2 = reg(g, "t2")
                gt = reg(g, "gt")
                first = (app == 0 and e == 0)
                if not first:
                    g.reg_add(t2, mp, -PE_T)
                if not first:
                    g.wait_ge(sems["s_pef"], mp2)
                w(g, "s_w1", cnt["s_w1"])
                inc(g.dma_start(out=b1cur[e % 2], in_=b1a[0:1, FF * e:FF * (e + 1)]), "s_w1", 16)
                w(g, "s_w2", cnt["s_w2"])
                inc(g.dma_start(out=b2cur[e % 2], in_=b2a[0:1, D * e:D * (e + 1)]), "s_w2", 16)
                for fc in range(32):
                    if not first:
                        g.reg_add(gt, t2, 3 if fc == 0 else 2 + 2 * fc)
                        g.wait_ge(sems["s_pef"], gt)
                    w(g, "s_w1", cnt["s_w1"])
                    inc(g.dma_start(
                        out=w1sb.rearrange("p (dc f) -> p dc f", f=FF)[:, :, 128 * fc:128 * (fc + 1)],
                        in_=w1a[e].rearrange("(dc p) f -> p dc f", p=128)[:, :, 128 * fc:128 * (fc + 1)]),
                        "s_w1", 16)
                    if not first:
                        g.reg_add(gt, t2, 2 * fc + 5)
                        g.wait_ge(sems["s_pef"], gt)
                    w(g, "s_w2", cnt["s_w2"])
                    inc(g.dma_start(out=w2sb[:, 1024 * fc:1024 * (fc + 1)],
                                    in_=w2a[e][128 * fc:128 * (fc + 1), :]), "s_w2", 16)
                g.reg_mov(mp2, mp)
                n = reg(g, "nt")
                g.reg_load(n, cnts_i[0:1, e:e + 1])
                g.reg_add(n, n, ETILE - 1)
                g.reg_div(n, n, ETILE)
                t1 = reg(g, "t1")
                g.reg_mul(t1, n, PE_T)
                g.reg_add(mp, mp, t1)
                ms = reg(g, "m_scf")
                g.reg_mul(t1, n, SCF_T)
                g.reg_add(ms, ms, t1)

            @block.sync
            def _(sy, e=e):
                n = reg(sy, "nt")
                sy.reg_load(n, cnts_i[0:1, e:e + 1])
                sy.reg_add(n, n, ETILE - 1)
                sy.reg_div(n, n, ETILE)
                off = reg(sy, "off")
                sy.reg_mov(off, e * SECT * D)
                mp = reg(sy, "m_pef")
                with _fori(nc, sy, "SP", 0, n):
                    sy.wait_ge(sems["s_pef"], mp)
                    sy.dma_start(out=xgtok[:].rearrange("p (a d) -> p a d", d=D),
                                 in_=bass.AP(xg_dram.tensor, off, [[D, 128], [128 * D, 2], [1, D]])
                                 ).then_inc(sems["s_ldf"], 16)
                    _free_dyn_regs(nc, "SP")
                    sy.reg_add(off, off, 256 * D)
                    sy.reg_add(mp, mp, PE_T)

            @block.tensor
            def _(t, e=e, w1_base=w1_base, w2_base=w2_base):
                n = reg(t, "nt")
                t.reg_load(n, cnts_i[0:1, e:e + 1])
                t.reg_add(n, n, ETILE - 1)
                t.reg_div(n, n, ETILE)
                m_ldf = reg(t, "m_ldf")
                m_actf = reg(t, "m_actf")
                m_dvef = reg(t, "m_dvef")
                m_dvef2 = reg(t, "m_dvef2")
                with _fori(nc, t, "PE", 0, n):
                    t.reg_add(m_ldf, m_ldf, LDF_T)
                    t.wait_ge(sems["s_ldf"], m_ldf)
                    for sblk in range(2):
                        t.wait_ge(sems["s_dvef"], m_dvef)
                        for dc in range(8):
                            tr = t.transpose(trp2[:, 128 * dc:128 * (dc + 1)],
                                             xgtok[:, 1024 * sblk + 128 * dc:1024 * sblk + 128 * (dc + 1)],
                                             ident_bf[:])
                            if dc == 7:
                                tr.then_inc(sems["s_pef"], 1)
                        t.reg_add(m_dvef, m_dvef, 1)
                    t.wait_ge(sems["s_dvef"], m_dvef)
                    t.wait_ge(sems["s_dvef2"], m_dvef2)
                    t.reg_add(m_dvef2, m_dvef2, 1)
                    for f in range(33):
                        if f <= 31:
                            fp = f % 2
                            t.wait_ge(sems["s_w1"], w1_base + 16 * (f + 2))
                            for dc in range(8):
                                t.matmul(hps[fp],
                                         w1sb.rearrange("p (dc f) -> p dc f", f=FF)[:, dc, 128 * f:128 * (f + 1)],
                                         xgT[:, 256 * dc:256 * (dc + 1)],
                                         start=(dc == 0), stop=False)
                            mm = t.matmul(hps[fp], b1cur[e % 2][0:1, 128 * f:128 * (f + 1)],
                                          ones_row_bf[0:1, 0:256], start=False, stop=True)
                            mm.then_inc(sems["s_pef"], 1)
                        if f >= 1:
                            g = f - 1
                            gp = g % 2
                            t.reg_add(m_actf, m_actf, 1)
                            t.wait_ge(sems["s_actf"], m_actf)
                            t.wait_ge(sems["s_w2"], w2_base + 16 * (g + 2))
                            if g == 0:
                                for sub in range(2):
                                    for dh in range(2):
                                        t.matmul(ops_[sub][:, 512 * dh:512 * (dh + 1)],
                                                 ones_row_bf[0:1, 0:128],
                                                 b2cur[e % 2][0:1, 512 * dh:512 * (dh + 1)],
                                                 start=True, stop=False)
                            mm2 = None
                            for sub in range(2):
                                for dh in range(2):
                                    mm2 = t.matmul(ops_[sub][:, 512 * dh:512 * (dh + 1)],
                                                   hb[gp][:, 128 * sub:128 * (sub + 1)],
                                                   w2sb[:, 1024 * g + 512 * dh:1024 * g + 512 * (dh + 1)],
                                                   start=False, stop=(g == 31))
                            mm2.then_inc(sems["s_pef"], 1)

            @block.scalar
            def _(s, e=e):
                n = reg(s, "nt")
                s.reg_load(n, cnts_i[0:1, e:e + 1])
                s.reg_add(n, n, ETILE - 1)
                s.reg_div(n, n, ETILE)
                m_pef = reg(s, "m_pef")
                m_dvef2 = reg(s, "m_dvef2")
                m_scf = reg(s, "m_scf")
                off = reg(s, "off")
                s.reg_mov(off, e * SECT * D)
                with _fori(nc, s, "Activation", 0, n):
                    s.reg_add(m_pef, m_pef, 2)
                    for f in range(32):
                        fp = f % 2
                        s.reg_add(m_pef, m_pef, 1 if f <= 1 else 2)
                        s.wait_ge(sems["s_pef"], m_pef)
                        s.activation(hb[fp], hps[fp], AF.Gelu).then_inc(sems["s_actf"], 1)
                    s.reg_add(m_pef, m_pef, 2)
                    s.reg_add(m_dvef2, m_dvef2, 1)
                    s.wait_ge(sems["s_dvef2"], m_dvef2)
                    s.dma_start(out=bass.AP(eo_dram.tensor, off, [[D, 128], [128 * D, 2], [1, D]]),
                                in_=eosb.rearrange("p (a d) -> p a d", d=D)
                                ).then_inc(sems["s_scf"], 16)
                    _free_dyn_regs(nc, "Activation")
                    s.reg_add(off, off, ETILE * D)
                    s.reg_add(m_scf, m_scf, 16)

            @block.vector
            def _(v, e=e):
                n = reg(v, "nt")
                v.reg_load(n, cnts_i[0:1, e:e + 1])
                v.reg_add(n, n, ETILE - 1)
                v.reg_div(n, n, ETILE)
                m_pef = reg(v, "m_pef")
                m_scf = reg(v, "m_scf")
                with _fori(nc, v, "DVE", 0, n):
                    for sblk in range(2):
                        v.reg_add(m_pef, m_pef, 1)
                        v.wait_ge(sems["s_pef"], m_pef)
                        v.tensor_copy(
                            xgT.rearrange("p (dc sl) -> p dc sl", sl=256)[:, :, 128 * sblk:128 * (sblk + 1)],
                            trp2.rearrange("p (dc sl) -> p dc sl", sl=128)
                            ).then_inc(sems["s_dvef"], 1)
                    v.reg_add(m_pef, m_pef, 64)
                    v.wait_ge(sems["s_pef"], m_pef)
                    v.wait_ge(sems["s_scf"], m_scf)
                    v.tensor_copy(eosb[:, 0:1024], ops_[0])
                    v.tensor_copy(eosb[:, 1024:2048], ops_[1]).then_inc(sems["s_dvef2"], 1)
                    v.reg_add(m_scf, m_scf, 16)

        @block.scalar
        def _(s):
            s.wait_ge(sems["s_scf"], reg(s, "m_scf"))
        @block.gpsimd
        def _(g):
            w(g, "s_g")
        barrier()

        escope(f"f{app}_exp", _sid)
        _sid = scope(f"f{app}_yadd")
        ya = [av(71168 + 1024 * k, 128, 1024) for k in (0, 1)]
        xa2 = [fsv(0, 128, 1024), fsv(1024, 128, 1024)]
        xn = [fsv(2048, 128, 1024), fsv(3072, 128, 1024)]

        for i in range(NTILE):
            pb = i % 2

            gsem = "s_g" if pb == 0 else "s_g2"
            lsem = "s_lda" if pb == 0 else "s_ldb"

            @block.gpsimd
            def _(g, i=i, pb=pb, gsem=gsem):
                if i >= 2:
                    w(g, "s_dve")
                inc(g.memset(ya[pb], 0.0), "s_gp")
                w(g, "s_gp")
                inc(g.indirect_dma_start(
                    out=ya[pb], out_offset=None,
                    in_=eo_dram, in_offset=IndirectOffsetOnAxis(gath_u[:, i:i + 1], 0),
                    bounds_check=reg(g, "bchk"), oob_is_err=False), gsem, 16)

            @block.sync
            def _(sy, i=i, pb=pb, lsem=lsem):
                if i >= 2:
                    w(sy, "s_dve")
                inc(sy.dma_start(out=xa2[pb], in_=x_src[128 * i:128 * (i + 1), :]), lsem, 16)

            @block.vector
            def _(v, i=i, pb=pb, gsem=gsem, lsem=lsem):
                w(v, gsem)
                w(v, lsem)
                if i >= 2:
                    w(v, "s_sc")
                inc(v.tensor_scalar(xn[pb], ya[pb], gsc_sb[:, i:i + 1], None, ALU.mult), "s_dve")
                w(v, "s_dve")
                inc(v.tensor_tensor(xn[pb], xn[pb], xa2[pb], ALU.add), "s_dve")

            @block.scalar
            def _(s, i=i, pb=pb):
                w(s, "s_dve")
                inc(s.dma_start(out=y_dst[128 * i:128 * (i + 1), :], in_=xn[pb]), "s_sc", 16)

        @block.scalar
        def _(s):
            w(s, "s_sc")
        barrier()
        escope(f"f{app}_yadd", _sid)

    def emit_attention(x_src, x_dst):
        _sid = scope("a_ln")
        ln2T = av(0, 128, 32768)
        emit_ln(x_src, with_logits=False, ln2T=ln2T)
        barrier()
        escope("a_ln", _sid)
        _sid = scope("a_kvq")

        wTsb = av(32768, 128, 8192)
        qstage = av(57344, 128, 4096)
        bv_row = av(82176, 1, D)
        bo_row = av(83200, 1, D)

        @block.gpsimd
        def _(g):
            inc(g.dma_start(out=bv_row, in_=bv), "s_g", 16)
            inc(g.dma_start(out=bo_row, in_=bo), "s_g", 16)

        def ln2T_rhs(dc, t0, ntok):
            i0 = t0 // 128
            ni = ntok // 128
            return ln2T.rearrange("p (i cq o) -> p i cq o", cq=8, o=128)[:, i0:i0 + ni, dc, :]

        @block.sync
        def _(sy):
            w(sy, "s_dve")
            inc(sy.dma_start(out=wTsb.rearrange("p (dc o) -> p dc o", o=D),
                             in_=wT_k.rearrange("(dc p) o -> p dc o", p=128)), "s_ld", 16)

        dve0 = cnt["s_dve"]
        for ch in range(8):
            for ts in range(8):
                bank = (ch * 8 + ts) % 2
                pcol = 1024 * bank

                @block.tensor
                def _(t, ch=ch, ts=ts, pcol=pcol):
                    w(t, "s_ld")
                    if ch * 8 + ts >= 2:
                        w(t, "s_dve", dve0 + ch * 8 + ts - 1)
                    for dc in range(8):
                        inc(t.matmul(ps[:, pcol:pcol + 512],
                                     wTsb[:, 1024 * dc + 128 * ch:1024 * dc + 128 * (ch + 1)],
                                     ln2T_rhs(dc, 512 * ts, 512),
                                     start=(dc == 0), stop=(dc == 7)), "s_pe")

                @block.vector
                def _(v, ch=ch, ts=ts, pcol=pcol):
                    w(v, "s_pe")
                    if ts == 0:
                        w(v, "s_sc")
                    inc(v.tensor_scalar(qstage[:, 512 * ts:512 * (ts + 1)],
                                        ps[:, pcol:pcol + 512], bk_sb[:, ch:ch + 1], None,
                                        ALU.add), "s_dve")

            @block.scalar
            def _(s, ch=ch):
                w(s, "s_dve")
                inc(s.dma_start(out=kt_in[128 * ch:128 * (ch + 1), :], in_=qstage), "s_sc", 16)

        @block.gpsimd
        def _(g):
            w(g, "s_sc")
            inc(g.collective_compute("AllGather", ALU.bypass,
                                     replica_groups=[[0, 1], [2, 3], [4, 5], [6, 7]],
                                     ins=[kt_in], outs=[kt_out]), "s_cc", 1)

        @block.sync
        def _(sy):
            w(sy, "s_pe")
            inc(sy.dma_start(out=wTsb.rearrange("p (dc o) -> p dc o", o=D),
                             in_=wT_v.rearrange("(dc p) o -> p dc o", p=128)), "s_ld", 16)

        dve0v = cnt["s_dve"]
        for ts in range(NTILE):
            for dh in range(2):
                k = ts * 2 + dh
                bank = k % 2
                pcol = 1024 * bank

                @block.tensor
                def _(t, ts=ts, dh=dh, pcol=pcol, k=k):
                    w(t, "s_ld")
                    w(t, "s_g")
                    w(t, "s_dve", dve0v + max(k - 1, 0))
                    for dc in range(8):
                        t.matmul(ps[:, pcol:pcol + 512],
                                 ln2T_rhs(dc, 128 * ts, 128),
                                 wTsb[:, 1024 * dc + 512 * dh:1024 * dc + 512 * (dh + 1)],
                                 start=(dc == 0), stop=False)
                    inc(t.matmul(ps[:, pcol:pcol + 512], ones_row_bf[0:1, 0:128],
                                 bv_row[0:1, 512 * dh:512 * (dh + 1)],
                                 start=False, stop=True), "s_pe")

                @block.vector
                def _(v, dh=dh, pcol=pcol):
                    w(v, "s_pe")
                    if dh == 0:
                        w(v, "s_sc")
                    inc(v.tensor_copy(qstage[:, 512 * dh:512 * (dh + 1)], ps[:, pcol:pcol + 512]), "s_dve")

            @block.scalar
            def _(s, ts=ts):
                w(s, "s_dve")
                inc(s.dma_start(out=v_in[128 * ts:128 * (ts + 1), :], in_=qstage[:, 0:1024]), "s_sc", 16)

        @block.gpsimd
        def _(g):
            w(g, "s_sc")
            inc(g.collective_compute("AllGather", ALU.bypass,
                                     replica_groups=[[0, 1], [2, 3], [4, 5], [6, 7]],
                                     ins=[v_in], outs=[v_out]), "s_cc", 1)

        @block.sync
        def _(sy):
            w(sy, "s_pe")
            inc(sy.dma_start(out=wTsb.rearrange("p (dc o) -> p dc o", o=D),
                             in_=wT_q.rearrange("(dc p) o -> p dc o", p=128)), "s_ld", 16)

        dve0q = cnt["s_dve"]
        for ch in range(8):
            for ts in range(8):
                bank = (ch * 8 + ts) % 2
                pcol = 1024 * bank

                @block.tensor
                def _(t, ch=ch, ts=ts, pcol=pcol):
                    w(t, "s_ld")
                    w(t, "s_dve", dve0q + max(ch * 8 + ts - 1, 0))
                    for dc in range(8):
                        inc(t.matmul(ps[:, pcol:pcol + 512],
                                     wTsb[:, 1024 * dc + 128 * ch:1024 * dc + 128 * (ch + 1)],
                                     ln2T_rhs(dc, 512 * ts, 512),
                                     start=(dc == 0), stop=(dc == 7)), "s_pe")

                @block.vector
                def _(v, ch=ch, ts=ts, pcol=pcol):
                    w(v, "s_pe")
                    if ts == 0:
                        w(v, "s_sc")
                    inc(v.tensor_scalar(qstage[:, 512 * ts:512 * (ts + 1)],
                                        ps[:, pcol:pcol + 512], bq_sb[:, ch:ch + 1], None,
                                        ALU.add), "s_dve")

            @block.scalar
            def _(s, ch=ch):
                w(s, "s_dve")
                inc(s.dma_start(out=qt_dram[128 * ch:128 * (ch + 1), :], in_=qstage), "s_sc", 16)

        barrier()

        escope("a_kvq", _sid)
        _sid = scope("a_core")
        attnT = av(0, 128, 32768)
        attnT_lo = arena[0:64, 0:32768]
        attnT_hi = arena[64:128, 0:32768]
        qt_sb = [av(32768, 128, 4096), av(36864, 128, 4096)]
        kt_sb = [av(40960, 128, 8192), av(49152, 128, 8192)]
        v_sb = [av(57344, 128, 8320), av(65664, 128, 8320)]
        P_sb = av(73984, 128, 2048)
        lb_sb = av(76032, 128, 1024, F32)
        linvAB = av(78080, 1, 2048, F32)

        SA = ps[:, 0:1024]
        SB = ps[:, 1024:2048]
        PVA = ps[0:65, 2048:3072]
        PVB = ps[0:65, 3072:4096]
        LBA = ps[0:64, 0:1024]
        LBB = ps[0:64, 1024:2048]

        @block.vector
        def _(v):
            inc(v.memset(v_sb[0][:], 1.0), "s_dve")
            inc(v.memset(v_sb[1][:], 1.0), "s_dve")
        ones_done = cnt["s_dve"]

        pe_done = {}
        for hp in range(NP):
            pb = hp % 2

            lsem = "s_lda" if pb == 0 else "s_ldb"

            @block.sync
            def _(sy, hp=hp, pb=pb, lsem=lsem):
                w(sy, "s_cc")
                if hp >= 2:
                    sy.wait_ge(sems["s_pe"], pe_done[hp - 2])
                if hp < 2:
                    w(sy, "s_dve", ones_done)
                inc(sy.dma_start(out=kt_sb[pb][:, 0:4096], in_=kt_out[128 * hp:128 * (hp + 1), :]), lsem, 16)
                inc(sy.dma_start(out=kt_sb[pb][:, 4096:8192],
                                 in_=kt_out[1024 + 128 * hp:1024 + 128 * (hp + 1), :]), lsem, 16)
                inc(sy.dma_start(out=qt_sb[pb][:], in_=qt_dram[128 * hp:128 * (hp + 1), :]), lsem, 16)
                for gblk in range(8):
                    for h in range(2):
                        inc(sy.dma_start(
                            out=v_sb[pb].rearrange("p (c j) -> p c j", j=130)[
                                :, 8 * gblk:8 * (gblk + 1), 65 * h:65 * h + 64],
                            in_=v_out.rearrange("(c p) d -> p c d", p=128)[
                                :, 8 * gblk:8 * (gblk + 1),
                                128 * hp + 64 * h:128 * hp + 64 * (h + 1)]), lsem, 16)

            ld_pair = cnt[lsem]

            for qt in range(NQT):
                act0 = cnt["s_act"]
                qa_mark = {}
                qb_mark = {}
                for c in range(NKC + 1):
                    @block.tensor
                    def _(t, hp=hp, pb=pb, qt=qt, c=c, ld_pair=ld_pair, lsem=lsem,
                          act0=act0):
                        if c == 0:
                            w(t, lsem, ld_pair)
                            w(t, "s_act")
                            w(t, "s_dve")
                        if c >= 1:
                            t.wait_ge(sems["s_act"], act0 + 2 * c - 1)
                            for j in range(2):
                                mmpv = t.matmul(PVA[:, 512 * j:512 * (j + 1)],
                                                v_sb[pb][:, 130 * (c - 1):130 * (c - 1) + 65],
                                                P_sb[:, 512 * j:512 * (j + 1)],
                                                start=(c == 1), stop=(c == NKC))
                            inc(mmpv, "s_pe")
                            t.wait_ge(sems["s_act"], act0 + 2 * c)
                            for j in range(2):
                                mmpv = t.matmul(PVB[:, 512 * j:512 * (j + 1)],
                                                v_sb[pb][:, 130 * (c - 1) + 65:130 * (c - 1) + 130],
                                                P_sb[:, 1024 + 512 * j:1024 + 512 * (j + 1)],
                                                start=(c == 1), stop=(c == NKC))
                            inc(mmpv, "s_pe")
                        if c <= NKC - 1:
                            qs = [(0, SA, 0), (1, SB, 0), (0, SA, 1), (1, SB, 1)]
                            for hh, So, j in qs:
                                mmqk = t.matmul(So[:, 512 * j:512 * (j + 1)],
                                                kt_sb[pb][64 * hh:64 * (hh + 1), 128 * c:128 * (c + 1)],
                                                qt_sb[pb][64 * hh:64 * (hh + 1),
                                                          1024 * qt + 512 * j:1024 * qt + 512 * (j + 1)],
                                                start=True, stop=True)
                                if (hh, j) == (0, 1):
                                    inc(mmqk, "s_pe")
                                    qa_mark[c] = cnt["s_pe"]
                                elif (hh, j) == (1, 1):
                                    inc(mmqk, "s_pe")
                                    qb_mark[c] = cnt["s_pe"]
                pv_end = cnt["s_pe"]

                @block.scalar
                def _(s, qa_mark=dict(qa_mark), qb_mark=dict(qb_mark)):
                    for c in range(NKC):
                        s.wait_ge(sems["s_pe"], qa_mark[c])
                        s.activation(P_sb[:, 0:1024], SA, AF.Exp,
                                     scale=0.125).then_inc(sems["s_act"], 1)
                        s.wait_ge(sems["s_pe"], qb_mark[c])
                        s.activation(P_sb[:, 1024:2048], SB, AF.Exp,
                                     scale=0.125).then_inc(sems["s_act"], 1)
                cnt["s_act"] += 2 * NKC

                @block.vector
                def _(v, pv_end=pv_end):
                    v.wait_ge(sems["s_pe"], pv_end)
                    inc(v.reciprocal(linvAB, ps[64:65, 2048:4096]), "s_dve")
                rc_mark = cnt["s_dve"]

                @block.tensor
                def _(t, rc_mark=rc_mark):
                    t.wait_ge(sems["s_dve"], rc_mark)
                    for j in range(2):
                        inc(t.matmul(LBA[:, 512 * j:512 * (j + 1)], ones64_f[:],
                                     linvAB[0:1, 512 * j:512 * (j + 1)], start=True, stop=True), "s_pe")
                    for j in range(2):
                        inc(t.matmul(LBB[:, 512 * j:512 * (j + 1)], ones64_f[:],
                                     linvAB[0:1, 1024 + 512 * j:1024 + 512 * (j + 1)],
                                     start=True, stop=True), "s_pe")

                @block.vector
                def _(v, hp=hp, qt=qt):
                    w(v, "s_pe")
                    col = 4096 * hp + 1024 * qt
                    inc(v.tensor_copy(lb_sb[0:64, :], LBA), "s_dve")
                    w(v, "s_dve")
                    inc(v.tensor_tensor(attnT_lo[:, col:col + 1024],
                                        ps[0:64, 2048:3072], lb_sb[0:64, :], ALU.mult), "s_dve")
                    w(v, "s_dve")
                    inc(v.tensor_copy(lb_sb[0:64, :], LBB), "s_dve")
                    w(v, "s_dve")
                    inc(v.tensor_tensor(attnT_hi[:, col:col + 1024],
                                        ps[0:64, 3072:4096], lb_sb[0:64, :], ALU.mult), "s_dve")

            pe_done[hp] = cnt["s_pe"]

        barrier()

        escope("a_core", _sid)
        _sid = scope("a_oproj")
        wTo_sb = av(84224, 128, 8192)
        xa3 = [fsv(0, 128, 1024), fsv(1024, 128, 1024)]
        xn3 = [fsv(2048, 128, 1024), fsv(3072, 128, 1024)]

        @block.sync
        def _(sy):
            inc(sy.dma_start(out=wTo_sb.rearrange("p (dc o) -> p dc o", o=D),
                             in_=wT_o.rearrange("(dc p) o -> p dc o", p=128)), "s_ld", 16)

        for i in range(NTILE):
            pb = i % 2
            pcol = 1024 * pb

            lsem = "s_lda" if pb == 0 else "s_ldb"

            @block.sync
            def _(sy, i=i, pb=pb, lsem=lsem):
                if i >= 2:
                    w(sy, "s_dve")
                inc(sy.dma_start(out=xa3[pb], in_=x_src[128 * i:128 * (i + 1), :]), lsem, 16)

            @block.tensor
            def _(t, i=i, pcol=pcol, lsem=lsem):
                w(t, "s_ld")
                w(t, lsem)
                if i >= 2:
                    w(t, "s_dve")
                for dh in range(2):
                    t.matmul(ps[:, pcol + 512 * dh:pcol + 512 * (dh + 1)],
                             ones_row_bf[0:1, 0:128], bo_row[0:1, 512 * dh:512 * (dh + 1)],
                             start=True, stop=False)
                    mm = None
                    for hp in range(NP):
                        mm = t.matmul(ps[:, pcol + 512 * dh:pcol + 512 * (dh + 1)],
                                      attnT[:, 4096 * hp + 128 * i:4096 * hp + 128 * (i + 1)],
                                      wTo_sb[:, 1024 * hp + 512 * dh:1024 * hp + 512 * (dh + 1)],
                                      start=False, stop=(hp == NP - 1))
                    inc(mm, "s_pe")

            @block.vector
            def _(v, i=i, pb=pb, pcol=pcol):
                w(v, "s_pe")
                if i >= 2:
                    w(v, "s_sc")
                inc(v.tensor_tensor(xn3[pb][:, 0:512], ps[:, pcol:pcol + 512], xa3[pb][:, 0:512], ALU.add), "s_dve")
                inc(v.tensor_tensor(xn3[pb][:, 512:1024], ps[:, pcol + 512:pcol + 1024],
                                    xa3[pb][:, 512:1024], ALU.add), "s_dve")

            @block.scalar
            def _(s, i=i, pb=pb):
                w(s, "s_dve")
                inc(s.dma_start(out=x_dst[128 * i:128 * (i + 1), :], in_=xn3[pb]), "s_sc", 16)

        @block.scalar
        def _(s):
            w(s, "s_sc")
        barrier()
        escope("a_oproj", _sid)

    escope("pre", _sid0)
    emit_ffn(0, x_in, x_sp1 if stage >= 2 else out, lg_in[0], lg_out[0])
    if stage >= 2:
        emit_attention(x_sp1, x_sp2 if stage >= 3 else out)
    if stage >= 3:
        emit_ffn(1, x_sp2, out, lg_in[1], lg_out[1])

    @block.scalar
    def _(s):
        w(s, "s_sc")

    es.close()
    return nc


def kernel(**inputs):
    x = np.ascontiguousarray(np.asarray(inputs["L0_attn_hidden_states"], dtype=np.float32))
    B, T, Dm = x.shape
    xf = x.reshape(NCORES, NT, Dm)
    Wqv = np.ascontiguousarray(inputs["L0_attn_Wq"], dtype=np.float32)
    Wkv = np.ascontiguousarray(inputs["L0_attn_Wk"], dtype=np.float32)
    Wvv = np.ascontiguousarray(inputs["L0_attn_Wv"], dtype=np.float32)
    Wov = np.ascontiguousarray(inputs["L0_attn_Wo"], dtype=np.float32)
    bqv = np.ascontiguousarray(np.asarray(inputs["L0_attn_bq"], dtype=np.float32).reshape(8, 128).T)
    bkv = np.ascontiguousarray(np.asarray(inputs["L0_attn_bk"], dtype=np.float32).reshape(8, 128).T)
    bvv = np.asarray(inputs["L0_attn_bv"], dtype=np.float32).reshape(1, D)
    bov = np.asarray(inputs["L0_attn_bo"], dtype=np.float32).reshape(1, D)
    rwv = np.ascontiguousarray(inputs["L0_ffn_router_w"], dtype=np.float32)
    w1v = np.ascontiguousarray(inputs["L0_ffn_w1"], dtype=np.float32)
    b1v = np.asarray(inputs["L0_ffn_b1"], dtype=np.float32).reshape(1, E * FF)
    w2v = np.ascontiguousarray(inputs["L0_ffn_w2"], dtype=np.float32)
    b2v = np.asarray(inputs["L0_ffn_b2"], dtype=np.float32).reshape(1, E * D)

    nc = build_nc(stage=3)
    in_maps = []
    for c in range(NCORES):
        cc = np.zeros((1, 8), np.float32)
        cc[0, 0] = c
        in_maps.append(dict(
            x_in=xf[c], Wq=Wqv, Wk=Wkv, Wv=Wvv, Wo=Wov, bq=bqv, bk=bkv, bv=bvv,
            bo=bov, rw=rwv, w1a=w1v, b1a=b1v, w2a=w2v, b2a=b2v, ccin=cc))
    res = run_bass_kernel_spmd(nc, in_maps, core_ids=list(range(NCORES)))
    outs = [res.results[c]["out"] for c in range(NCORES)]
    return np.concatenate(outs, axis=0).reshape(B, T, Dm)
